# version 04
# baseline (speedup 1.0000x reference)
import numpy as np
import ml_dtypes
from contextlib import ExitStack

import concourse.bass as bass
import concourse.mybir as mybir
from concourse.bass_utils import run_bass_kernel_spmd

F32 = mybir.dt.float32
BF16 = mybir.dt.bfloat16
I16 = mybir.dt.int16
ALU = mybir.AluOpType
AF = mybir.ActivationFunctionType
AX = mybir.AxisListType

T, C, H, D, INNER = 2048, 1024, 16, 64, 4096
NT = T // 128
NB = 4
BT = NT // NB
NJ = INNER // 512
NKC = C // 128
NFD = 4
GH = 2
NG = H // GH

EPS_LN, EPS_RMS = 1e-5, 1e-6
MSK = -2000.0
INV_C = 1.0 / C
INV_WN = 1.0 / (1024 * 4096)
TWO_PI = 6.283185307179586
C_HI = float(np.float32(TWO_PI))
C_LO = float(np.float64(TWO_PI) - np.float64(np.float32(TWO_PI)))
INV2PI = 1.0 / TWO_PI
PI_ = float(np.pi)

DO_A, DO_B, DO_C = True, True, True
DBG = False
CUT = None


class Sem:
    def __init__(self, st, nc, name):
        self.h = st.enter_context(nc.semaphore(name))
        self.n = 0


class Plan:
    def __init__(self):
        self.ops = []
    def on(self, eng, fn):
        self.ops.append((eng, fn))
    def op(self, eng, sem, k, fn):
        self.ops.append((eng, lambda e, h=sem.h, k=k, fn=fn: fn(e).then_inc(h, k)))
        sem.n += k
        return sem.n
    def wait(self, eng, sem, val=None):
        v = sem.n if val is None else val
        if v > 0:
            self.ops.append((eng, lambda e, h=sem.h, v=v: e.wait_ge(h, v)))
    def run(self, key, e):
        for k, fn in self.ops:
            if k == key:
                fn(e)


def host_prepare(w_qkv, w_out, q_norm_w, k_norm_w, w_gate, w_up, w_down):
    f32 = np.float32
    out = {}
    wgT = np.asarray(w_gate, f32).T
    wuT = np.asarray(w_up, f32).T
    wdT = np.asarray(w_down, f32).T
    for nm, w in (("wg", wgT), ("wu", wuT)):
        out[nm] = np.ascontiguousarray(w.reshape(NKC, 128, NJ, 512).transpose(2, 1, 0, 3))
    out["wd"] = np.ascontiguousarray(wdT.reshape(32, 128, NFD, 256).transpose(2, 1, 0, 3))
    wqkvT = np.asarray(w_qkv, f32).T
    out["wqkv"] = np.ascontiguousarray(wqkvT.reshape(NKC, 128, 24, 128).transpose(2, 1, 0, 3))
    woT = np.asarray(w_out, f32).T
    out["wo"] = np.ascontiguousarray(woT.reshape(NKC, 128, 2, 512).transpose(2, 1, 0, 3))

    p = np.arange(128)
    out["iota_t"] = np.broadcast_to(np.arange(T, dtype=np.int16), (128, T)).copy()
    w64 = 10000.0 ** (-(p % 32).astype(np.float64) / 32.0)
    whi = w64.astype(f32)
    wlo = (w64 - whi.astype(np.float64)).astype(f32)
    cvec = np.stack([
        whi, wlo,
        np.where((p % 64) < 32, -1.0, 1.0).astype(f32),
        np.asarray(q_norm_w, f32)[p % 64],
        np.asarray(k_norm_w, f32)[p % 64],
        np.full(128, EPS_RMS, f32),
        np.full(128, -8.0, f32),
        np.full(128, EPS_LN, f32),
    ], axis=1)
    out["cvec"] = np.ascontiguousarray(cvec)
    out["ident"] = np.eye(128, dtype=f32).astype(ml_dtypes.bfloat16)
    strips = np.zeros((4, 128, 512), f32)
    ff = np.arange(512)[None, :]
    for sgm in range(4):
        strips[sgm] = np.where(p[:, None] + 128 * sgm > ff, MSK, 0.0)
    out["strips"] = strips.astype(ml_dtypes.bfloat16)
    blk = np.zeros((128, 2), f32); blk[:64, 0] = 1; blk[64:, 1] = 1
    bw = np.zeros((128, 128), f32)
    bw[:64, :64] = 1; bw[64:, 64:] = 1
    out["blkw"] = bw.astype(ml_dtypes.bfloat16)
    out["blk"] = blk.astype(ml_dtypes.bfloat16)
    out["sel2f"] = np.ascontiguousarray(blk.T)
    out["ones128"] = np.ones((128, 128), f32)
    sw = (p // 64) * 64 + ((p + 32) % 64)
    pm = np.zeros((128, 128), f32)
    pm[sw, p] = 1.0
    out["pswap"] = pm.astype(ml_dtypes.bfloat16)
    sh = np.zeros((64, 128), f32)
    sh[np.arange(64), np.arange(64) + 64] = 1.0
    out["shift64"] = sh.astype(ml_dtypes.bfloat16)
    out["ones64f"] = np.ones((1, 64), f32)
    return out


def build_program():
    nc = bass.Bass("TRN2", target_bir_lowering=False, debug=False, detect_race_conditions=False)
    st = ExitStack()
    g = {}
    din = lambda n, sh, dt=F32: nc.dram_tensor(n, sh, dt, kind="ExternalInput").ap()
    g["x_h"] = din("x", [T, C])
    g["wg_h"] = din("wg", [NJ, 128, NKC, 512])
    g["wu_h"] = din("wu", [NJ, 128, NKC, 512])
    g["wd_h"] = din("wd", [NFD, 128, 32, 256])
    g["wqkv_h"] = din("wqkv", [24, 128, NKC, 128])
    g["wo_h"] = din("wo", [2, 128, NKC, 512])
    g["iota_h"] = din("iota_t", [128, T], I16)
    g["cvec_h"] = din("cvec", [128, 8])
    g["ident_h"] = din("ident", [128, 128], BF16)
    g["pswap_h"] = din("pswap", [128, 128], BF16)
    g["strips_h"] = din("strips", [4, 128, 512], BF16)
    g["blk_h"] = din("blk", [128, 2], BF16)
    g["blkw_h"] = din("blkw", [128, 128], BF16)
    g["shift64_h"] = din("shift64", [64, 128], BF16)
    g["ones64f_h"] = din("ones64f", [1, 64])
    g["sel2f_h"] = din("sel2f", [2, 128])
    g["ones128_h"] = din("ones128", [128, 128])
    g["out_h"] = nc.dram_tensor("out", [T, C], F32, kind="ExternalOutput").ap()
    g["tg_h"] = nc.dram_tensor("tern_g", [NJ, 128, NKC, 512], BF16, kind="Internal").ap()
    g["tu_h"] = nc.dram_tensor("tern_u", [NJ, 128, NKC, 512], BF16, kind="Internal").ap()
    g["td_h"] = nc.dram_tensor("tern_d", [NFD, 128, 32, 256], BF16, kind="Internal").ap()
    if DBG:
        g["dbg_hq"] = nc.dram_tensor("dbg_hq", [128, NKC, NT, 128], BF16, kind="ExternalOutput").ap()
        g["dbg_vec"] = nc.dram_tensor("dbg_vec", [128, 8 * NT], F32, kind="ExternalOutput").ap()
        g["dbg_h"] = nc.dram_tensor("dbg_h", [128, C], F32, kind="ExternalOutput").ap()
        g["dbg_vec2"] = nc.dram_tensor("dbg_vec2", [128, 8 * NT], F32, kind="ExternalOutput").ap()
        g["dbg_vt"] = nc.dram_tensor("dbg_vt", [128, 16], F32, kind="ExternalOutput").ap()
        g["dbg_wsc"] = nc.dram_tensor("dbg_wsc", [128, 8], F32, kind="ExternalOutput").ap()
        g["dbg_wg"] = nc.dram_tensor("dbg_wg", [128, 16, 512], BF16, kind="ExternalOutput").ap()
        g["dbg_arena"] = nc.dram_tensor("dbg_arena", [128, NJ, BT, 512], BF16, kind="ExternalOutput").ap()
        g["dbg_pq"] = nc.dram_tensor("dbg_pq", [128, NJ, BT, 512], BF16, kind="ExternalOutput").ap()

    S = lambda n, sh, dt=F32: st.enter_context(nc.sbuf_tensor(n, sh, dt))
    g["X"] = S("X", [128, NT, C])
    g["cvec"] = S("cvec_s", [128, 8])
    g["vec"] = S("vec", [128, 8 * NT])
    g["vt"] = S("vt", [128, 24])
    g["wsc"] = S("wsc", [128, 8])
    g["ident"] = S("ident_s", [128, 128], BF16)
    g["pswap"] = S("pswap_s", [128, 128], BF16)
    g["strips"] = S("strips_s", [128, 4, 512], BF16)
    g["blk"] = S("blk_s", [128, 2], BF16)
    g["blkw"] = S("blkw_s", [128, 128], BF16)
    g["shift64"] = S("shift64_s", [64, 128], BF16)
    g["ones64f"] = S("ones64f_s", [1, 64])
    g["sel2f"] = S("sel2f_s", [2, 128])
    g["ones128"] = S("ones128_s", [128, 128])
    g["iota"] = S("iota_s", [128, T], I16)

    OVR = 129 * 1024
    base, _end = nc.bump_sbuf(OVR)
    ovr_off = {}
    def at(name, shape, dtype=F32, o=None):
        nbytes = int(np.prod(shape[1:])) * mybir.dt.size(dtype)
        assert o is not None
        o = base + o
        assert o + nbytes <= base + OVR, f"overlay overflow at {name}: {o-base}+{nbytes}"
        ovr_off[name] = o - base
        return nc.alloc_sbuf_tensor_at(name, list(shape), dtype, offset=o)

    K = 1024
    g["Hq_fm"] = at("Hq_fm", [128, NKC, NT, 128], BF16, o=0)
    g["arena"] = at("arena", [128, NJ, BT, 512], BF16, o=32 * K)
    g["wbuf"] = at("wbuf", [128, 16, 512], BF16, o=64 * K)
    g["wdbuf"] = at("wdbuf", [128, 32, 256], BF16, o=64 * K)
    g["f32buf"] = at("f32buf", [128, NKC, 512], F32, o=80 * K)
    g["f32b2"] = at("f32b2", [128, NKC, 512], F32, o=64 * K)
    g["wbuf2"] = at("wbuf2", [128, 16, 512], BF16, o=80 * K)
    g["wdbuf2"] = at("wdbuf2", [128, 32, 256], BF16, o=80 * K)
    g["pqtmp"] = at("pqtmp", [128, BT, 512], BF16, o=96 * K)
    g["pqtmp2"] = at("pqtmp2", [128, BT, 512], BF16, o=124 * K + 256)
    g["hscr"] = at("hscr", [128, 2, C], F32, o=100 * K)
    g["i16s"] = at("i16s", [128, C], I16, o=108 * K)
    g["d16s"] = at("d16s", [128, C], BF16, o=110 * K)
    g["t16s"] = at("t16s", [128, C], BF16, o=112 * K)
    g["sqd"] = at("sqd", [128, C], BF16, o=114 * K)
    g["gtmp"] = at("gtmp", [128, 2, 512], BF16, o=116 * K)
    g["hqt"] = at("hqt", [128, 2, C], BF16, o=118 * K)
    g["i16b"] = at("i16b", [128, C], I16, o=122 * K)

    g["ru"] = at("ru", [128, T], F32, o=0)
    g["rk"] = at("rk", [128, T], I16, o=8 * K)
    g["rd"] = at("rd", [128, T], BF16, o=12 * K)
    g["rt"] = at("rt", [128, T], BF16, o=16 * K)
    g["rr"] = at("rr", [128, T], F32, o=20 * K)
    g["cosT"] = at("cosT", [128, T], F32, o=32 * K)
    g["sinT"] = at("sinT", [128, T], F32, o=40 * K)
    g["qkw"] = at("qkw", [128, NKC, 256], BF16, o=48 * K)
    g["vw"] = at("vw", [128, NKC, 128], BF16, o=52 * K)
    g["expS"] = at("expS", [128, 2, 512], BF16, o=54 * K)
    g["smr"] = at("smr", [2, 512], F32, o=56 * K)
    g["rn2"] = at("rn2", [2, 512], F32, o=58 * K)
    g["sqb"] = at("sqb", [128, 2, 512], BF16, o=60 * K)
    g["qksb"] = at("qksb", [128, 2, 512], BF16, o=62 * K)
    g["h2fm"] = at("h2fm", [128, NKC, NT, 128], BF16, o=80 * K)
    g["attn_fm"] = at("attn_fm", [128, NKC, NT, 128], BF16, o=0)
    g["wob"] = at("wob", [128, NKC, 512], BF16, o=112 * K)
    g["Qn"] = at("Qn", [128, T], BF16, o=112 * K)
    g["Kn"] = at("Kn", [128, T], BF16, o=116 * K)
    g["Vext"] = at("Vext", [128, NT, GH * 65], BF16, o=120 * K)
    g["h2stg"] = at("h2stg", [128, 2, C], BF16, o=124 * K + 256)
    g["ra"] = at("ra", [128, 2, 512], F32, o=64 * K)
    g["rb"] = at("rb", [128, 512], F32, o=56 * K + 512)
    g["pb"] = [st.enter_context(nc.psum_tensor(f"pb{i}", [128, 512], F32)) for i in range(8)]
    return nc, st, g


def emit_all(nc, st, g):
    P = Plan()
    s = {}
    for nm in ("x", "cst", "wf", "wf2", "twb", "trd", "trd2", "qw", "tp_hq", "tp_h2", "tp_at",
               "tp_pq", "te", "v", "a", "stm"):
        s[nm] = Sem(st, nc, "s_" + nm)
    pb = g["pb"]
    X, cvec, vec, vt, wsc = g["X"], g["cvec"], g["vec"], g["vt"], g["wsc"]
    VC = lambda b, ti: vec[:, b * NT + ti: b * NT + ti + 1]
    VTc = lambda i: vt[:, i:i + 1]
    CV = lambda i: cvec[:, i:i + 1]
    vsep = lambda e: e.tensor_copy(vt[:, 23:24], cvec[:, 0:1])

    xv = g["x_h"].rearrange("(i p) c -> p i c", p=128)
    P.op('sy', s["x"], 16, lambda e: e.dma_start(out=X[:], in_=xv))
    for dst, src in ((g["cvec"][:], g["cvec_h"]), (g["ident"][:], g["ident_h"]),
                     (g["pswap"][:], g["pswap_h"]),
                     (g["strips"][:], g["strips_h"].rearrange("s p f -> p s f")),
                     (g["blk"][:], g["blk_h"]), (g["blkw"][:], g["blkw_h"]),
                     (g["shift64"][:], g["shift64_h"]), (g["ones64f"][:], g["ones64f_h"]),
                     (g["sel2f"][:], g["sel2f_h"]),
                     (g["ones128"][:], g["ones128_h"]), (g["iota"][:], g["iota_h"])):
        P.op('sy', s["cst"], 16, lambda e, d=dst, sr=src: e.dma_start(out=d, in_=sr))

    def dbg_dump(name, src_ap, eng='sy'):
        if not DBG or name not in g:
            return
        P.wait(eng, s["v"])
        P.wait(eng, s["a"])
        P.wait(eng, s["te"])
        P.op(eng, s["stm"], 16, lambda e: e.dma_start(out=g[name], in_=src_ap))
        for k in ('sy', 'v', 'a', 't', 'gp'):
            P.wait(k, s["stm"])

    def init0(e):
        e.memset(vt[:], 0.0)
        e.memset(vec[:], 0.0)
        return e.memset(wsc[:], 0.0)
    P.op('v', s["v"], 1, init0)

    absum_guard = {"v": 0}

    def absum_units(src_h, col):
        nch = src_h.shape[0]
        per = int(np.prod(src_h.shape[2:]))
        nsub = (per + 4095) // 4096
        units = []
        state = {"acc": 0, "a_prev": [None, None]}
        idx = [0]
        for ch in range(nch):
            for sub in range(nsub):
                n = min(4096, per - sub * 4096)
                nq = n // C
                def unit(ch=ch, sub=sub, n=n, nq=nq):
                    i = idx[0]; idx[0] += 1
                    buf = g["f32buf"] if i % 2 == 0 else g["f32b2"]
                    wfs = s["wf"] if i % 2 == 0 else s["wf2"]
                    fb = buf[:].rearrange("p a b -> p (a b)")
                    if state["a_prev"][i % 2] is None:
                        P.wait('gp', s["a"])
                    else:
                        P.wait('gp', s["a"], state["a_prev"][i % 2])
                    P.op('gp', wfs, 16, lambda e, fb=fb: e.dma_start(
                        out=fb[:, :n],
                        in_=src_h[ch].rearrange("p a b -> p (a b)")[:, sub * 4096: sub * 4096 + n]))
                    P.wait('a', wfs)
                    P.wait('a', s["cst"])
                    P.wait('a', s["v"], absum_guard["v"])
                    base = 96 + state["acc"]
                    def absacc(e, nq=nq, base=base, fb=fb):
                        for q in range(nq):
                            inst = e.activation(out=g["sqd"][:], in_=fb[:, q * C:(q + 1) * C],
                                                func=AF.Abs, accum_out=vec[:, base + q: base + q + 1])
                        return inst
                    state["a_prev"][i % 2] = P.op('a', s["a"], 1, absacc)
                    state["acc"] += nq
                units.append(unit)
        def finalize():
            assert state["acc"] == 32
            P.wait('v', s["a"])
            absum_guard["v"] = P.op('v', s["v"], 1,
                lambda e: e.reduce_sum(VTc(14), vec[:, 96:128], axis=AX.X))
            P.wait('t', s["v"])
            P.wait('t', s["cst"])
            P.op('t', s["te"], 1, lambda e: e.matmul(pb[7][:, 0:1], g["ones128"][:], VTc(14),
                                                     start=True, stop=True))
            P.wait('v', s["te"])
            def fin(e, col=col):
                e.tensor_scalar(wsc[:, col:col + 1], pb[7][:, 0:1], INV_WN, 1e-5, ALU.mult, ALU.max)
                vsep(e)
                return e.reciprocal(wsc[:, col + 3:col + 4], wsc[:, col:col + 1])
            P.op('v', s["v"], 1, fin)
        return units, finalize

    def emit_weight_quant_chunk(src_h, ch, dst_sb, tern_h, col):
        f32buf = g["f32buf"]
        per = int(np.prod(src_h.shape[2:]))
        fb = f32buf[:].rearrange("p a b -> p (a b)")
        dst = dst_sb.rearrange("p a b -> p (a b)")
        nsub = (per + 4095) // 4096
        for sub in range(nsub):
            n = min(4096, per - sub * 4096)
            P.wait('gp', s["v"])
            P.op('gp', s["wf"], 16, lambda e, ch=ch, sub=sub, n=n: e.dma_start(
                out=fb[:, :n],
                in_=src_h[ch].rearrange("p a b -> p (a b)")[:, sub * 4096: sub * 4096 + n]))
            for q in range(n // C):
                sl = slice(q * C, (q + 1) * C)
                osl = slice(sub * 4096 + q * C, sub * 4096 + (q + 1) * C)
                P.wait('a', s["wf"])
                P.wait('a', s["v"])
                P.op('a', s["a"], 1, lambda e, sl=sl, col=col: e.activation(
                    out=g["i16s"][:], in_=fb[:, sl], func=AF.Copy,
                    scale=wsc[:, col + 3:col + 4]))
                P.wait('v', s["a"])
                def rnd(e, sl=sl, osl=osl, col=col):
                    e.scalar_tensor_tensor(g["d16s"][:], fb[:, sl], wsc[:, col + 3:col + 4],
                                           g["i16s"][:], ALU.mult, ALU.subtract)
                    e.scalar_tensor_tensor(g["t16s"][:], g["d16s"][:], 0.5, g["i16s"][:],
                                           ALU.is_gt, ALU.add)
                    e.scalar_tensor_tensor(g["d16s"][:], g["d16s"][:], -0.5, g["t16s"][:],
                                           ALU.is_lt, ALU.subtract)
                    return e.tensor_scalar(dst[:, osl], g["d16s"][:], -1.0, 1.0, ALU.max, ALU.min)
                P.op('v', s["v"], 1, rnd)
        P.wait('gp', s["v"])
        P.op('gp', s["twb"], 16, lambda e, ch=ch: e.dma_start(
            out=tern_h[ch].rearrange("p a b -> p (a b)"), in_=dst[:, :per]))

    ln_state = {"vt_free": [0, 0], "hb_free": [0, 0], "i16_free": [0, 0],
                "rnd_at": {}, "apply_at": {}}

    def emit_ln_pair(p0, quantize):
        st_ = ln_state
        xg = st_.get("x_guard", 0)
        tiles = [p0, p0 + 1]
        V = lambda n, it: vt[:, n + 8 * (it % 2): n + 8 * (it % 2) + 1]
        sq_at, stats_at, sqrt_at, rstd_at, qs_at, i16_at = {}, {}, {}, {}, {}, {}
        dump = g["sqd"][:] if quantize else g["rr"][:, 0:C]
        for it in tiles:
            P.wait('a', s["x"])
            P.wait('a', s["v"], max(xg, st_["vt_free"][it % 2]))
            sq_at[it] = P.op('a', s["a"], 1, lambda e, it=it: e.activation(
                out=dump, in_=X[:, it, :], func=AF.Square, accum_out=V(1, it)))
        for it in tiles:
            P.wait('v', s["a"], sq_at[it])
            def stats(e, it=it):
                e.reduce_sum(V(2, it), X[:, it, :], axis=AX.X)
                e.tensor_scalar(V(4, it), V(1, it), INV_C, None, ALU.mult)
                e.tensor_scalar(V(3, it), V(2, it), INV_C, None, ALU.mult)
                vsep(e)
                return e.scalar_tensor_tensor(V(5, it), V(3, it), V(3, it), V(4, it),
                                              ALU.mult, ALU.subtract)
            stats_at[it] = P.op('v', s["v"], 1, stats)
        for it in tiles:
            P.wait('a', s["v"], stats_at[it])
            sqrt_at[it] = P.op('a', s["a"], 1, lambda e, it=it: e.activation(
                out=V(6, it), in_=V(5, it), func=AF.Sqrt, scale=-1.0, bias=CV(7)))
        for it in tiles:
            P.wait('v', s["a"], sqrt_at[it])
            def rstd(e, it=it):
                e.reciprocal(V(7, it), V(6, it))
                vsep(e)
                return e.scalar_tensor_tensor(V(8, it), V(3, it), -1.0, V(7, it),
                                              ALU.mult, ALU.mult)
            rstd_at[it] = P.op('v', s["v"], 1, rstd)
        if not quantize:
            for it in tiles:
                h2t = g["h2stg"][:, it % 2, :]
                P.wait('a', s["v"], rstd_at[it])
                if it >= 2:
                    P.wait('a', s["tp_h2"], 16 * (it - 1))
                a_ap = P.op('a', s["a"], 1, lambda e, it=it, h2t=h2t: e.activation(
                    out=h2t, in_=X[:, it, :], func=AF.Identity, scale=V(7, it), bias=V(8, it)))
                P.wait('sy', s["a"], a_ap)
                P.wait('sy', s["tp_hq"])
                P.wait('sy', s["tp_pq"])
                P.wait('sy', s["tp_at"])
                P.op('sy', s["tp_h2"], 16, lambda e, it=it, h2t=h2t: e.dma_start_transpose(
                    g["h2fm"][:, :, it, :], h2t))
                st_["vt_free"][it % 2] = rstd_at[it]
            return
        for it in tiles:
            hb = g["hscr"][:, it % 2, :]
            P.wait('a', s["v"], rstd_at[it])
            P.wait('a', s["v"], st_["hb_free"][it % 2])
            st_["apply_at"][it] = P.op('a', s["a"], 1, lambda e, it=it, hb=hb: e.activation(
                out=hb, in_=X[:, it, :], func=AF.Identity, scale=V(7, it), bias=V(8, it)))
        for it in tiles:
            hb = g["hscr"][:, it % 2, :]
            P.wait('v', s["a"], st_["apply_at"][it])
            def qscale(e, it=it, hb=hb):
                e.reduce_max(V(1, it), hb, axis=AX.X, apply_absolute_value=True)
                vsep(e)
                e.tensor_scalar(VC(5, it), V(1, it), 1e-5, None, ALU.max)
                vsep(e)
                e.reciprocal(V(3, it), VC(5, it))
                vsep(e)
                return e.tensor_scalar(VC(3, it), V(3, it), 127.0, None, ALU.mult)
            qs_at[it] = P.op('v', s["v"], 1, qscale)
            st_["vt_free"][it % 2] = qs_at[it]
        for it in tiles:
            hb = g["hscr"][:, it % 2, :]
            i16 = g["i16s"] if it % 2 == 0 else g["i16b"]
            P.wait('a', s["v"], qs_at[it])
            P.wait('a', s["v"], st_["i16_free"][it % 2])
            i16_at[it] = P.op('a', s["a"], 1, lambda e, hb=hb, it=it, i16=i16: e.activation(
                out=i16[:], in_=hb, func=AF.Identity, scale=VC(3, it)))
        for it in tiles:
            hb = g["hscr"][:, it % 2, :]
            i16 = g["i16s"] if it % 2 == 0 else g["i16b"]
            hq = g["hqt"][:, it % 2, :]
            P.wait('v', s["a"], i16_at[it])
            if it >= 2:
                P.wait('v', s["tp_hq"], 16 * (it - 1))
            def rnd(e, hb=hb, it=it, hq=hq, i16=i16):
                e.scalar_tensor_tensor(g["d16s"][:], hb, VC(3, it), i16[:], ALU.mult, ALU.subtract)
                e.scalar_tensor_tensor(g["t16s"][:], g["d16s"][:], 0.5, i16[:], ALU.is_gt, ALU.add)
                return e.scalar_tensor_tensor(hq, g["d16s"][:], -0.5, g["t16s"][:], ALU.is_lt, ALU.subtract)
            n = P.op('v', s["v"], 1, rnd)
            ln_state["rnd_at"][it] = n
            st_["hb_free"][it % 2] = n
            st_["i16_free"][it % 2] = n
            P.wait('sy', s["v"], n)
            P.wait('sy', s["tp_pq"])
            P.wait('sy', s["tp_h2"])
            P.wait('sy', s["tp_at"])
            P.op('sy', s["tp_hq"], 16, lambda e, it=it, hq=hq: e.dma_start_transpose(
                g["Hq_fm"][:, :, it, :], hq))

    def emit_ln(quantize):
        ln_state["vt_free"] = [0, 0]
        ln_state["hb_free"] = [0, 0]
        ln_state["i16_free"] = [0, 0]
        ln_state["x_guard"] = s["v"].n
        for p0 in range(0, NT, 2):
            emit_ln_pair(p0, quantize)

    def emit_gu(stage, b):
        wbuf, gtmp = g["wbuf"], g["gtmp"]
        quant = (stage == 0 and b == 0)
        bank_free = [s["a"].n, s["a"].n, s["v"].n, s["v"].n]
        gt_free = [s["v"].n, s["v"].n]
        slot_te = [s["te"].n, s["te"].n]
        v_guard = s["v"].n
        for j in range(NJ):
            wb = wbuf if (quant or j % 2 == 0) else g["wbuf2"]
            gslot = wb[:, 0:NKC, :]
            uslot = wb[:, NKC:2 * NKC, :]
            if quant:
                if j > 0:
                    P.wait('v', s["te"])
                    P.wait('v', s["twb"])
                emit_weight_quant_chunk(g["wg_h"], j, gslot, g["tg_h"], 0)
                emit_weight_quant_chunk(g["wu_h"], j, uslot, g["tu_h"], 1)
                slot_rdy = ('v', s["v"].n, 0)
            else:
                trds = s["trd"] if j % 2 == 0 else s["trd2"]
                P.wait('gp', s["te"], slot_te[j % 2])
                if j < 2:
                    P.wait('gp', s["v"], v_guard)
                P.op('gp', trds, 16, lambda e, j=j, gslot=gslot: e.dma_start(
                    out=gslot.rearrange("p a b -> p (a b)"),
                    in_=g["tg_h"][j].rearrange("p a b -> p (a b)")))
                P.op('gp', trds, 16, lambda e, j=j, uslot=uslot: e.dma_start(
                    out=uslot.rearrange("p a b -> p (a b)"),
                    in_=g["tu_h"][j].rearrange("p a b -> p (a b)")))
                slot_rdy = ('trd', trds, trds.n)
            if slot_rdy[0] == 'v':
                P.wait('t', s["v"], slot_rdy[1])
            else:
                P.wait('t', slot_rdy[1], slot_rdy[2])
            P.wait('t', s["tp_hq"])
            for k in range(BT):
                ti = b * BT + k
                gbank, ubank = pb[k % 2], pb[2 + (k % 2)]
                P.wait('t', s["a"], bank_free[k % 2])
                def gmm(e, ti=ti, gbank=gbank, gslot=gslot):
                    for c in range(NKC):
                        inst = e.matmul(gbank[:], g["Hq_fm"][:, c, ti, :], gslot[:, c, :],
                                        start=(c == 0), stop=(c == NKC - 1))
                    return inst
                te_g = P.op('t', s["te"], 1, gmm)
                gt = gtmp[:, k % 2, :]
                P.wait('a', s["te"], te_g)
                P.wait('a', s["v"], gt_free[k % 2])
                bank_free[k % 2] = P.op('a', s["a"], 1,
                    lambda e, gt=gt, ti=ti, gbank=gbank: e.activation(
                        out=gt, in_=gbank[:], func=AF.Silu, scale=VC(0, ti)))
                P.wait('t', s["v"], bank_free[2 + k % 2])
                def umm(e, ti=ti, ubank=ubank, uslot=uslot):
                    for c in range(NKC):
                        inst = e.matmul(ubank[:], g["Hq_fm"][:, c, ti, :], uslot[:, c, :],
                                        start=(c == 0), stop=(c == NKC - 1))
                    return inst
                te_u = P.op('t', s["te"], 1, umm)
                P.wait('v', s["te"], te_u)
                P.wait('v', s["a"], bank_free[k % 2])
                n = P.op('v', s["v"], 1, lambda e, j=j, k=k, ti=ti, ubank=ubank, gt=gt:
                         e.scalar_tensor_tensor(g["arena"][:, j, k, :], ubank[:], VC(1, ti),
                                                gt, ALU.mult, ALU.mult))
                bank_free[2 + k % 2] = n
                gt_free[k % 2] = n
            slot_te[j % 2] = s["te"].n

    def emit_p(stage, b, tp_pq_prev):
        arena, pqtmp = g["arena"], g["pqtmp"]
        for k in range(BT):
            ti = b * BT + k
            P.wait('v', s["v"])
            def q2s(e, k=k, ti=ti):
                e.reduce_max(VTc(1), arena[:, :, k, :], axis=AX.XY, apply_absolute_value=True)
                vsep(e)
                e.tensor_scalar(VTc(2), VTc(1), 1e-5, None, ALU.max)
                vsep(e)
                e.reciprocal(VTc(3), VTc(2))
                e.scalar_tensor_tensor(VC(2, ti), VTc(2), 0.5 / 127.0, wsc[:, 2:3],
                                       ALU.mult, ALU.mult)
                return e.tensor_scalar(VC(4, ti), VTc(3), 127.0, None, ALU.mult)
            P.op('v', s["v"], 1, q2s)
        q2_done = s["v"].n
        seq = [(j, k) for j in range(NJ) for k in range(BT)]
        a_at, v_at = {}, {}
        def emit_cast(n):
            j, k = seq[n]
            ti = b * BT + k
            i16 = g["i16s"][:, (n % 2) * 512:(n % 2) * 512 + 512]
            P.wait('a', s["v"], max(q2_done, v_at.get(n - 2, 0)))
            a_at[n] = P.op('a', s["a"], 1, lambda e, j=j, k=k, ti=ti, i16=i16: e.activation(
                out=i16, in_=arena[:, j, k, :], func=AF.Identity, scale=VC(4, ti)))
        def emit_rnd(n):
            j, k = seq[n]
            ti = b * BT + k
            i16 = g["i16s"][:, (n % 2) * 512:(n % 2) * 512 + 512]
            pqt = g["pqtmp"] if j % 2 == 0 else g["pqtmp2"]
            P.wait('v', s["a"], a_at[n])
            if k == 0:
                P.wait('v', s["tp_pq"], tp_pq_prev[j % 2])
            def rnd(e, j=j, k=k, ti=ti, i16=i16, pqt=pqt):
                e.scalar_tensor_tensor(g["d16s"][:, 0:512], arena[:, j, k, :], VC(4, ti),
                                       i16, ALU.mult, ALU.subtract)
                e.scalar_tensor_tensor(g["t16s"][:, 0:512], g["d16s"][:, 0:512], 0.5,
                                       i16, ALU.is_gt, ALU.add)
                return e.scalar_tensor_tensor(pqt[:, k, :], g["d16s"][:, 0:512], -0.5,
                                              g["t16s"][:, 0:512], ALU.is_lt, ALU.subtract)
            v_at[n] = P.op('v', s["v"], 1, rnd)
            if k == BT - 1:
                P.wait('sy', s["v"], v_at[n])
                P.wait('sy', s["tp_pq"])
                P.wait('sy', s["tp_hq"])
                P.wait('sy', s["tp_h2"])
                P.wait('sy', s["tp_at"])
                tp_pq_prev[j % 2] = s["tp_pq"].n + 16
                P.op('sy', s["tp_pq"], 16, lambda e, j=j, pqt=pqt: e.dma_start_transpose(
                    arena[:, j].rearrange("p k (c q) -> p k c q", c=4), pqt[:]))
        for n in range(len(seq) + 1):
            if n < len(seq):
                emit_cast(n)
            if n >= 1:
                emit_rnd(n - 1)

    def emit_d(stage, b):
        wdbuf, arena = g["wdbuf"], g["arena"]
        quant = (stage == 0 and b == 0)
        bank_free = [s["v"].n, s["v"].n]
        dslot_te = [s["te"].n, s["te"].n]
        v_guard_d = s["v"].n
        for fc in range(NFD):
            wdb = wdbuf if (quant or fc % 2 == 0) else g["wdbuf2"]
            if quant:
                P.wait('v', s["te"])
                P.wait('v', s["twb"])
                emit_weight_quant_chunk(g["wd_h"], fc, wdb[:], g["td_h"], 2)
                P.wait('t', s["v"])
            else:
                trds = s["trd"] if fc % 2 == 0 else s["trd2"]
                P.wait('gp', s["te"], dslot_te[fc % 2])
                if fc < 2:
                    P.wait('gp', s["v"], v_guard_d)
                P.op('gp', trds, 16, lambda e, fc=fc, wdb=wdb: e.dma_start(
                    out=wdb[:].rearrange("p a b -> p (a b)"),
                    in_=g["td_h"][fc].rearrange("p a b -> p (a b)")))
                P.wait('t', trds)
            P.wait('t', s["tp_pq"])
            slotv = arena[:].rearrange("p j k (c q) -> p j k c q", c=4)
            for k in range(BT):
                ti = b * BT + k
                ybank = pb[4 + (k % 2)]
                P.wait('t', s["v"], bank_free[k % 2])
                def dmm(e, k=k, ybank=ybank, slotv=slotv, fc=fc, wdb=wdb):
                    for kc in range(32):
                        inst = e.matmul(ybank[:, 0:256], slotv[:, kc // 4, k, kc % 4, :],
                                        wdb[:, kc, :], start=(kc == 0), stop=(kc == 31))
                    return inst
                te_y = P.op('t', s["te"], 1, dmm)
                xsl = X[:, ti, fc * 256:(fc + 1) * 256]
                P.wait('v', s["te"], te_y)
                bank_free[k % 2] = P.op('v', s["v"], 1,
                    lambda e, ybank=ybank, ti=ti, xsl=xsl: e.scalar_tensor_tensor(
                        xsl, ybank[:, 0:256], VC(2, ti), xsl, ALU.mult, ALU.add))
            dslot_te[fc % 2] = s["te"].n

    def emit_ffn(stage):
        if stage == 0:
            units = []
            for srch, col in ((g["wg_h"], 0), (g["wu_h"], 1), (g["wd_h"], 2)):
                u, fin = absum_units(srch, col)
                units.append((u, fin))
            ln_state["vt_free"] = [0, 0]
            ln_state["hb_free"] = [0, 0]
            ln_state["i16_free"] = [0, 0]
            ln_state["x_guard"] = s["v"].n
            flat = [u for us, _ in units for u in us]
            fins = {len(units[0][0]) - 1: units[0][1],
                    len(units[0][0]) + len(units[1][0]) - 1: units[1][1],
                    len(units[0][0]) + len(units[1][0]) + len(units[2][0]) - 1: units[2][1]}
            pair_i = [0]
            for i, u in enumerate(flat):
                u()
                if i in fins:
                    fins[i]()
                if i % 3 == 2 and pair_i[0] < NT // 2:
                    emit_ln_pair(2 * pair_i[0], True)
                    pair_i[0] += 1
            while pair_i[0] < NT // 2:
                emit_ln_pair(2 * pair_i[0], True)
                pair_i[0] += 1
        else:
            emit_ln(quantize=True)
        def batch_scales(e):
            vsep(e)
            e.tensor_scalar(vec[:, 0:NT], vec[:, 5 * NT:6 * NT], wsc[:, 0:1], 1.0 / 127.0,
                            ALU.mult, ALU.mult)
            return e.tensor_scalar(vec[:, NT:2 * NT], vec[:, 5 * NT:6 * NT], wsc[:, 1:2],
                                   1.0 / 127.0, ALU.mult, ALU.mult)
        P.op('v', s["v"], 1, batch_scales)
        if CUT == 'absum' or CUT == 'ln':
            if DBG:
                dbg_dump("dbg_wsc", wsc[:])
                dbg_dump("dbg_hq", g["Hq_fm"][:])
                dbg_dump("dbg_vec", vec[:])
            return
        tp_pq_prev = [0, 0]
        for b in range(NB):
            emit_gu(stage, b)
            if stage == 0 and b == 0 and DBG:
                dbg_dump("dbg_wg", g["wbuf"][:])
                dbg_dump("dbg_arena", g["arena"][:])
            if CUT == 'gu0' and b == 0:
                return
            emit_p(stage, b, tp_pq_prev)
            if stage == 0 and b == 0 and DBG:
                dbg_dump("dbg_pq", g["arena"][:])
                dbg_dump("dbg_vec2", vec[:])
            if CUT == 'p0' and b == 0:
                return
            emit_d(stage, b)
            if stage == 0 and b == 0 and CUT == 'd0':
                return

    def emit_rope_tables():
        iota, cosT, sinT = g["iota"], g["cosT"], g["sinT"]
        ru, rk, rd, rt, rr = g["ru"], g["rk"], g["rd"], g["rt"], g["rr"]
        P.wait('v', s["cst"])
        def theta(e):
            e.tensor_scalar(ru[:], iota[:], CV(1), None, ALU.mult)
            return e.scalar_tensor_tensor(rr[:], iota[:], CV(0), ru[:], ALU.mult, ALU.add)
        P.op('v', s["v"], 1, theta)
        for phase, dst, signed in ((0.0, sinT, True), (float(np.pi / 2), cosT, False)):
            def reduce2(e, phase=phase):
                e.tensor_scalar(ru[:], rr[:], INV2PI, phase * INV2PI, ALU.mult, ALU.add)
                e.tensor_copy(rk[:], ru[:])
                e.tensor_tensor(rd[:], ru[:], rk[:], ALU.subtract)
                e.scalar_tensor_tensor(rt[:], rd[:], 0.5, rk[:], ALU.is_gt, ALU.add)
                e.scalar_tensor_tensor(rt[:], rd[:], -0.5, rt[:], ALU.is_lt, ALU.subtract)
                e.scalar_tensor_tensor(ru[:], rt[:], C_HI, rr[:], ALU.mult, ALU.add)
                e.tensor_scalar(ru[:], ru[:], 1.0, phase, ALU.mult, ALU.add)
                e.scalar_tensor_tensor(ru[:], rt[:], C_LO, ru[:], ALU.mult, ALU.add)
                return e.tensor_scalar(ru[:], ru[:], -PI_, PI_, ALU.max, ALU.min)
            P.op('v', s["v"], 1, reduce2)
            P.wait('a', s["v"])
            P.op('a', s["a"], 1, lambda e, dst=dst: e.activation(
                out=dst[:], in_=ru[:], func=AF.Sin))
            P.wait('v', s["a"])
            if signed:
                P.op('v', s["v"], 1, lambda e, dst=dst: e.tensor_scalar(
                    dst[:], dst[:], CV(2), None, ALU.mult))

    def emit_attn():
        av_evac = {}
        emit_rope_tables()
        emit_ln(quantize=False)
        strips = g["strips"]
        for grp in range(NG):
            P.wait('gp', s["te"])
            P.op('gp', s["qw"], 16, lambda e, grp=grp: e.dma_start(
                out=g["qkw"][:, :, 0:128], in_=g["wqkv_h"][grp]))
            P.op('gp', s["qw"], 16, lambda e, grp=grp: e.dma_start(
                out=g["qkw"][:, :, 128:256], in_=g["wqkv_h"][NG + grp]))
            qw_rdy = P.op('gp', s["qw"], 16, lambda e, grp=grp: e.dma_start(
                out=g["vw"][:], in_=g["wqkv_h"][2 * NG + grp]))
            iters = [(sel, tch) for sel in (0, 1) for tch in range(4)]
            qk_free = [s["v"].n, s["v"].n]
            st_free = [s["a"].n, s["a"].n]
            sw_free = [s["v"].n, s["v"].n]
            rn_free = [s["v"].n, s["v"].n]
            te_qk = {}
            a_evac = {}
            te_stats = {}
            a_rn = {}
            def emit_qk_mm(i):
                sel, tch = iters[i]
                qk = pb[i % 2]
                P.wait('t', s["qw"], qw_rdy)
                P.wait('t', s["tp_h2"])
                P.wait('t', s["v"], qk_free[i % 2])
                def qkmm(e, sel=sel, tch=tch, qk=qk):
                    rhs = g["h2fm"][:, :, 4 * tch:4 * tch + 4, :]
                    for c in range(NKC):
                        inst = e.matmul(qk[:], g["qkw"][:, c, sel * 128:(sel + 1) * 128],
                                        rhs[:, c].rearrange("p i q -> p (i q)"),
                                        start=(c == 0), stop=(c == NKC - 1))
                    return inst
                te_qk[i] = P.op('t', s["te"], 1, qkmm)
            def emit_evac(i):
                sel, tch = iters[i]
                qk = pb[i % 2]
                stb, swb, rnb = pb[2 + i % 2], pb[4 + i % 2], pb[6 + i % 2]
                sqb = g["sqb"][:, i % 2, :]
                qksb = g["qksb"][:, i % 2, :]
                P.wait('a', s["te"], te_qk[i])
                P.wait('a', s["te"], te_stats.get(i - 2, 0))
                P.wait('a', s["v"], av_evac.get("v", 0))
                P.op('a', s["a"], 1, lambda e, qk=qk, sqb=sqb: e.activation(
                    out=sqb, in_=qk[:], func=AF.Square))
                a_evac[i] = P.op('a', s["a"], 1, lambda e, qk=qk, qksb=qksb: e.activation(
                    out=qksb, in_=qk[:], func=AF.Copy))
                P.wait('t', s["a"], a_evac[i])
                P.wait('t', s["a"], st_free[i % 2])
                P.op('t', s["te"], 1, lambda e, stb=stb, sqb=sqb: e.matmul(
                    stb[:], g["blkw"][:], sqb, start=True, stop=True))
                P.wait('t', s["v"], sw_free[i % 2])
                te_stats[i] = P.op('t', s["te"], 1, lambda e, swb=swb, qksb=qksb: e.matmul(
                    swb[:], g["pswap"][:], qksb, start=True, stop=True))
                P.wait('a', s["te"], te_stats[i])
                P.wait('a', s["v"], rn_free[i % 2])
                P.op('a', s["a"], 1, lambda e, stb=stb, rnb=rnb: e.activation(
                    out=rnb[:], in_=stb[:], func=AF.Ln, scale=1.0 / 64.0, bias=CV(5)))
                st_free[i % 2] = s["a"].n
                a_rn[i] = P.op('a', s["a"], 1, lambda e, rnb=rnb: e.activation(
                    out=rnb[:], in_=rnb[:], func=AF.Exp, scale=-0.5))
                dstbuf = g["Qn"] if sel == 0 else g["Kn"]
                wcol = 3 if sel == 0 else 4
                tsl = slice(tch * 512, (tch + 1) * 512)
                ra = g["ra"][:, i % 2, :]
                P.wait('v', s["a"], a_rn[i])
                def rope(e, qk=qk, swb=swb, rnb=rnb, tsl=tsl, wcol=wcol, dstbuf=dstbuf, ra=ra):
                    e.tensor_tensor(ra, qk[:], g["cosT"][:, tsl], ALU.mult)
                    e.tensor_tensor(g["rb"][:], swb[:], g["sinT"][:, tsl], ALU.mult)
                    e.tensor_add(ra, ra, g["rb"][:])
                    return e.scalar_tensor_tensor(dstbuf[:, tsl], ra, CV(wcol), rnb[:],
                                                  ALU.mult, ALU.mult)
                n = P.op('v', s["v"], 1, rope)
                qk_free[i % 2] = n
                sw_free[i % 2] = n
                rn_free[i % 2] = n
            for i in range(len(iters) + 1):
                if i < len(iters):
                    emit_qk_mm(i)
                if i >= 1:
                    emit_evac(i - 1)
            for ti in range(NT):
                vbank = pb[ti % 2]
                P.wait('t', s["v"])
                def vmm(e, ti=ti, vbank=vbank):
                    for c in range(NKC):
                        inst = e.matmul(vbank[:, 0:128], g["h2fm"][:, c, ti, :],
                                        g["vw"][:, c, :], start=(c == 0), stop=(c == NKC - 1))
                    return inst
                te_v = P.op('t', s["te"], 1, vmm)
                P.wait('v', s["te"], te_v)
                def vevac(e, ti=ti, vbank=vbank):
                    e.tensor_copy(g["Vext"][:, ti, :].rearrange("p (h d) -> p h d", d=65)[:, :, 0:64],
                                  vbank[:, 0:128].rearrange("p (h d) -> p h d", d=64))
                    return e.memset(g["Vext"][:, ti, 64::65], 1.0)
                P.op('v', s["v"], 1, vevac)
            for e_i in range(2):
                head = 2 * grp + e_i
                for tch in range(4):
                    ntk = 4 * (tch + 1)
                    qsl = slice(tch * 512, (tch + 1) * 512)
                    avb = pb[2 + ((2 * grp + e_i) * 4 + tch) % 2]
                    exp_at = {}
                    av_at = {}
                    def emit_s(tkt):
                        diag = tkt - 4 * tch
                        sb = pb[tkt % 2]
                        if tkt >= 2:
                            P.wait('t', s["a"], exp_at[tkt - 2])
                        ksl = slice(tkt * 128, (tkt + 1) * 128)
                        def smm(e, e_i=e_i, ksl=ksl, qsl=qsl, sb=sb, diag=diag):
                            i1 = e.matmul(sb[:], g["Kn"][64 * e_i:64 * (e_i + 1), ksl],
                                          g["Qn"][64 * e_i:64 * (e_i + 1), qsl],
                                          start=True, stop=(diag < 0))
                            if diag >= 0:
                                i1 = e.matmul(sb[:], g["ident"][:], strips[:, diag, :],
                                              start=False, stop=True)
                            return i1
                        te_s = P.op('t', s["te"], 1, smm)
                        P.wait('a', s["te"], te_s)
                        if tkt >= 2:
                            P.wait('a', s["te"], av_at[tkt - 2])
                        P.op('a', s["a"], 1, lambda e, tkt=tkt, sb=sb: e.activation(
                            out=g["expS"][:, tkt % 2, :], in_=sb[:], func=AF.Exp,
                            scale=0.125, bias=CV(6)))
                        exp_at[tkt] = s["a"].n
                    def emit_av(tkt):
                        P.wait('t', s["a"], exp_at[tkt])
                        if tkt == 0:
                            P.wait('t', s["v"], av_evac.get("v", 0))
                        def avmm(e, e_i=e_i, tkt=tkt, ntk=ntk, avb=avb):
                            return e.matmul(avb[:, 0:65].rearrange("p n -> p n") if False else avb[0:65, :],
                                            g["Vext"][:, tkt, e_i * 65:(e_i + 1) * 65],
                                            g["expS"][:, tkt % 2, :],
                                            start=(tkt == 0), stop=(tkt == ntk - 1))
                        av_at[tkt] = P.op('t', s["te"], 1, avmm)
                    for tkt in range(ntk + 1):
                        if tkt < ntk:
                            emit_s(tkt)
                        if tkt >= 1:
                            emit_av(tkt - 1)
                    P.wait('a', s["te"], av_at[ntk - 1])
                    lsb = g["rb"][0:1, :]
                    P.wait('a', s["v"], av_evac.get("v", 0))
                    P.op('a', s["a"], 1, lambda e, avb=avb: e.activation(
                        out=lsb, in_=avb[64:65, :], func=AF.Ln))
                    a_ln = P.op('a', s["a"], 1, lambda e: e.activation(
                        out=lsb, in_=lsb, func=AF.Exp, scale=-1.0))
                    lb_i = 6 + ((2 * grp + e_i) * 4 + tch) % 2
                    lbb = pb[lb_i]
                    P.wait('t', s["a"], a_ln)
                    P.wait('t', s["v"], av_evac.get(("lb", lb_i), 0))
                    te_lb = P.op('t', s["te"], 1, lambda e, lbb=lbb: e.matmul(
                        lbb[0:64, :], g["ones64f"][:], lsb, start=True, stop=True))
                    P.wait('v', s["te"], te_lb)
                    linvb = g["ra"][0:64, ((2 * grp + e_i) * 4 + tch) % 2, :]
                    P.op('v', s["v"], 1, lambda e, lbb=lbb, linvb=linvb: e.tensor_copy(linvb, lbb[0:64, :]))
                    av_evac[("lb", lb_i)] = s["v"].n
                    dst = g["attn_fm"][(head % 2) * 64:(head % 2) * 64 + 64, head // 2,
                                       4 * tch:4 * tch + 4, :]
                    if head % 2 == 0:
                        n = P.op('v', s["v"], 1, lambda e, avb=avb, linvb=linvb, dst=dst: e.tensor_tensor(
                            dst.rearrange("p i q -> p (i q)"), avb[0:64, :], linvb, ALU.mult))
                        av_evac["v"] = n
                    else:
                        nsb = g["qksb"][:, 0, :][0:64, :]
                        P.op('v', s["v"], 1, lambda e, avb=avb, linvb=linvb, nsb=nsb: e.tensor_tensor(
                            nsb, avb[0:64, :], linvb, ALU.mult))
                        v_ns = s["v"].n
                        prm_i = 6 + ((2 * grp + e_i) * 4 + tch + 1) % 2
                        prm = pb[prm_i]
                        P.wait('t', s["v"], max(v_ns, av_evac.get(("lb", prm_i), 0)))
                        te_pm = P.op('t', s["te"], 1, lambda e, nsb=nsb, prm=prm: e.matmul(
                            prm[:], g["shift64"][:], nsb, start=True, stop=True))
                        P.wait('v', s["te"], te_pm)
                        n = P.op('v', s["v"], 1, lambda e, prm=prm, dst=dst: e.tensor_copy(
                            dst.rearrange("p i q -> p (i q)"), prm[64:128, :]))
                        av_evac["v"] = n
                        av_evac[("lb", prm_i)] = n
        bank_free = [s["v"].n, s["v"].n]
        for fc in range(2):
            P.wait('gp', s["te"])
            qw_rdy = P.op('gp', s["qw"], 16, lambda e, fc=fc: e.dma_start(
                out=g["wob"][:], in_=g["wo_h"][fc]))
            for it in range(NT):
                yb = pb[2 + (it % 2)]
                P.wait('t', s["qw"], qw_rdy)
                P.wait('t', s["tp_at"])
                P.wait('t', s["v"], bank_free[it % 2])
                def omm(e, it=it, yb=yb):
                    for c in range(NKC):
                        inst = e.matmul(yb[:], g["attn_fm"][:, c, it, :], g["wob"][:, c, :],
                                        start=(c == 0), stop=(c == NKC - 1))
                    return inst
                te_y = P.op('t', s["te"], 1, omm)
                xsl = X[:, it, fc * 512:(fc + 1) * 512]
                P.wait('v', s["te"], te_y)
                bank_free[it % 2] = P.op('v', s["v"], 1,
                    lambda e, yb=yb, xsl=xsl: e.tensor_add(xsl, xsl, yb[:]))

    if DO_A:
        emit_ffn(0)
    if DO_B:
        emit_attn()
    if DO_C:
        emit_ffn(2)
    ov = g["out_h"].rearrange("(i p) c -> p i c", p=128)
    P.wait('sy', s["v"])
    P.wait('sy', s["a"])
    P.wait('sy', s["te"])
    P.op('sy', s["stm"], 16, lambda e: e.dma_start(out=ov, in_=X[:]))
    P.wait('sy', s["stm"])

    with nc.Block() as block:
        _fill_block(block, P)
    return P


def _fill_block(block, P):
    @block.sync
    def _(e):
        P.run('sy', e)
    @block.gpsimd
    def _(e):
        P.run('gp', e)
    @block.vector
    def _(e):
        P.run('v', e)
    @block.scalar
    def _(e):
        P.run('a', e)
    @block.tensor
    def _(e):
        P.run('t', e)


_NC_CACHE = {}

def _get_nc():
    key = (DO_A, DO_B, DO_C, CUT, DBG)
    if key not in _NC_CACHE:
        nc, st, g = build_program()
        emit_all(nc, st, g)
        _NC_CACHE[key] = (nc, st, g)
    return _NC_CACHE[key]


def kernel(L0_attn_x, L0_attn_w_qkv, L0_attn_w_out, L0_attn_q_norm_w,
           L0_attn_k_norm_w, L0_ffn_w_gate, L0_ffn_w_up, L0_ffn_w_down):
    consts = host_prepare(L0_attn_w_qkv, L0_attn_w_out, L0_attn_q_norm_w,
                          L0_attn_k_norm_w, L0_ffn_w_gate, L0_ffn_w_up, L0_ffn_w_down)
    x = np.ascontiguousarray(np.asarray(L0_attn_x, np.float32))
    nc, _st, _g = _get_nc()
    in_maps = []
    for b in range(8):
        m = {"x": np.ascontiguousarray(x[b])}
        m.update(consts)
        in_maps.append(m)
    res = run_bass_kernel_spmd(nc, in_maps, core_ids=list(range(8)))
    out = np.stack([np.asarray(r["out"]) for r in res.results], axis=0)
    return out.astype(np.float32)


# version 05
# speedup vs baseline: 1.1898x; 1.1898x over previous
import numpy as np
import ml_dtypes
from contextlib import ExitStack

import concourse.bass as bass
import concourse.mybir as mybir
from concourse.bass_utils import run_bass_kernel_spmd

F32 = mybir.dt.float32
BF16 = mybir.dt.bfloat16
I16 = mybir.dt.int16
ALU = mybir.AluOpType
AF = mybir.ActivationFunctionType
AX = mybir.AxisListType

T, C, H, D, INNER = 2048, 1024, 16, 64, 4096
NT = T // 128
NB = 4
BT = NT // NB
NJ = INNER // 512
NKC = C // 128
NFD = 4
GH = 2
NG = H // GH

EPS_LN, EPS_RMS = 1e-5, 1e-6
MSK = -2000.0
INV_C = 1.0 / C
INV_WN = 1.0 / (1024 * 4096)
TWO_PI = 6.283185307179586
C_HI = float(np.float32(TWO_PI))
C_LO = float(np.float64(TWO_PI) - np.float64(np.float32(TWO_PI)))
INV2PI = 1.0 / TWO_PI
PI_ = float(np.pi)

DO_A, DO_B, DO_C = True, True, True
DBG = False
CUT = None


class Sem:
    def __init__(self, st, nc, name):
        self.h = st.enter_context(nc.semaphore(name))
        self.n = 0


class Plan:
    def __init__(self):
        self.ops = []
    def on(self, eng, fn):
        self.ops.append((eng, fn))
    def op(self, eng, sem, k, fn):
        self.ops.append((eng, lambda e, h=sem.h, k=k, fn=fn: fn(e).then_inc(h, k)))
        sem.n += k
        return sem.n
    def wait(self, eng, sem, val=None):
        v = sem.n if val is None else val
        if v > 0:
            self.ops.append((eng, lambda e, h=sem.h, v=v: e.wait_ge(h, v)))
    def run(self, key, e):
        for k, fn in self.ops:
            if k == key:
                fn(e)


def host_prepare(w_qkv, w_out, q_norm_w, k_norm_w, w_gate, w_up, w_down):
    f32 = np.float32
    out = {}
    wgT = np.asarray(w_gate, f32).T
    wuT = np.asarray(w_up, f32).T
    wdT = np.asarray(w_down, f32).T
    for nm, w in (("wg", wgT), ("wu", wuT)):
        out[nm] = np.ascontiguousarray(w.reshape(NKC, 128, NJ, 512).transpose(2, 1, 0, 3))
    out["wd"] = np.ascontiguousarray(wdT.reshape(32, 128, NFD, 256).transpose(2, 1, 0, 3))
    wqkvT = np.asarray(w_qkv, f32).T
    out["wqkv"] = np.ascontiguousarray(wqkvT.reshape(NKC, 128, 24, 128).transpose(2, 1, 0, 3))
    woT = np.asarray(w_out, f32).T
    out["wo"] = np.ascontiguousarray(woT.reshape(NKC, 128, 2, 512).transpose(2, 1, 0, 3))

    p = np.arange(128)
    out["iota_t"] = np.broadcast_to(np.arange(T, dtype=np.int16), (128, T)).copy()
    w64 = 10000.0 ** (-(p % 32).astype(np.float64) / 32.0)
    whi = w64.astype(f32)
    wlo = (w64 - whi.astype(np.float64)).astype(f32)
    cvec = np.stack([
        whi, wlo,
        np.where((p % 64) < 32, -1.0, 1.0).astype(f32),
        np.asarray(q_norm_w, f32)[p % 64],
        np.asarray(k_norm_w, f32)[p % 64],
        np.full(128, EPS_RMS, f32),
        np.full(128, -8.0, f32),
        np.full(128, EPS_LN, f32),
    ], axis=1)
    out["cvec"] = np.ascontiguousarray(cvec)
    out["ident"] = np.eye(128, dtype=f32).astype(ml_dtypes.bfloat16)
    strips = np.zeros((4, 128, 512), f32)
    ff = np.arange(512)[None, :]
    for sgm in range(4):
        strips[sgm] = np.where(p[:, None] + 128 * sgm > ff, MSK, 0.0)
    out["strips"] = strips.astype(ml_dtypes.bfloat16)
    blk = np.zeros((128, 2), f32); blk[:64, 0] = 1; blk[64:, 1] = 1
    bw = np.zeros((128, 128), f32)
    bw[:64, :64] = 1; bw[64:, 64:] = 1
    out["blkw"] = bw.astype(ml_dtypes.bfloat16)
    out["blk"] = blk.astype(ml_dtypes.bfloat16)
    out["sel2f"] = np.ascontiguousarray(blk.T)
    out["ones128"] = np.ones((128, 128), f32)
    sw = (p // 64) * 64 + ((p + 32) % 64)
    pm = np.zeros((128, 128), f32)
    pm[sw, p] = 1.0
    out["pswap"] = pm.astype(ml_dtypes.bfloat16)
    sh = np.zeros((64, 128), f32)
    sh[np.arange(64), np.arange(64) + 64] = 1.0
    out["shift64"] = sh.astype(ml_dtypes.bfloat16)
    out["ones64f"] = np.ones((1, 64), f32)
    return out


def build_program():
    nc = bass.Bass("TRN2", target_bir_lowering=False, debug=False, detect_race_conditions=False)
    st = ExitStack()
    g = {}
    din = lambda n, sh, dt=F32: nc.dram_tensor(n, sh, dt, kind="ExternalInput").ap()
    g["x_h"] = din("x", [T, C])
    g["wg_h"] = din("wg", [NJ, 128, NKC, 512])
    g["wu_h"] = din("wu", [NJ, 128, NKC, 512])
    g["wd_h"] = din("wd", [NFD, 128, 32, 256])
    g["wqkv_h"] = din("wqkv", [24, 128, NKC, 128])
    g["wo_h"] = din("wo", [2, 128, NKC, 512])
    g["iota_h"] = din("iota_t", [128, T], I16)
    g["cvec_h"] = din("cvec", [128, 8])
    g["ident_h"] = din("ident", [128, 128], BF16)
    g["pswap_h"] = din("pswap", [128, 128], BF16)
    g["strips_h"] = din("strips", [4, 128, 512], BF16)
    g["blk_h"] = din("blk", [128, 2], BF16)
    g["blkw_h"] = din("blkw", [128, 128], BF16)
    g["shift64_h"] = din("shift64", [64, 128], BF16)
    g["ones64f_h"] = din("ones64f", [1, 64])
    g["sel2f_h"] = din("sel2f", [2, 128])
    g["ones128_h"] = din("ones128", [128, 128])
    g["out_h"] = nc.dram_tensor("out", [T, C], F32, kind="ExternalOutput").ap()
    g["tg_h"] = nc.dram_tensor("tern_g", [NJ, 128, NKC, 512], BF16, kind="Internal").ap()
    g["tu_h"] = nc.dram_tensor("tern_u", [NJ, 128, NKC, 512], BF16, kind="Internal").ap()
    g["td_h"] = nc.dram_tensor("tern_d", [NFD, 128, 32, 256], BF16, kind="Internal").ap()
    if DBG:
        g["dbg_hq"] = nc.dram_tensor("dbg_hq", [128, NKC, NT, 128], BF16, kind="ExternalOutput").ap()
        g["dbg_vec"] = nc.dram_tensor("dbg_vec", [128, 8 * NT], F32, kind="ExternalOutput").ap()
        g["dbg_h"] = nc.dram_tensor("dbg_h", [128, C], F32, kind="ExternalOutput").ap()
        g["dbg_vec2"] = nc.dram_tensor("dbg_vec2", [128, 8 * NT], F32, kind="ExternalOutput").ap()
        g["dbg_vt"] = nc.dram_tensor("dbg_vt", [128, 16], F32, kind="ExternalOutput").ap()
        g["dbg_wsc"] = nc.dram_tensor("dbg_wsc", [128, 8], F32, kind="ExternalOutput").ap()
        g["dbg_wg"] = nc.dram_tensor("dbg_wg", [128, 16, 512], BF16, kind="ExternalOutput").ap()
        g["dbg_arena"] = nc.dram_tensor("dbg_arena", [128, NJ, BT, 512], BF16, kind="ExternalOutput").ap()
        g["dbg_pq"] = nc.dram_tensor("dbg_pq", [128, NJ, BT, 512], BF16, kind="ExternalOutput").ap()

    S = lambda n, sh, dt=F32: st.enter_context(nc.sbuf_tensor(n, sh, dt))
    g["X"] = S("X", [128, NT, C])
    g["cvec"] = S("cvec_s", [128, 8])
    g["vec"] = S("vec", [128, 8 * NT])
    g["vt"] = S("vt", [128, 24])
    g["wsc"] = S("wsc", [128, 8])
    g["ident"] = S("ident_s", [128, 128], BF16)
    g["pswap"] = S("pswap_s", [128, 128], BF16)
    g["strips"] = S("strips_s", [128, 4, 512], BF16)
    g["blk"] = S("blk_s", [128, 2], BF16)
    g["blkw"] = S("blkw_s", [128, 128], BF16)
    g["shift64"] = S("shift64_s", [64, 128], BF16)
    g["ones64f"] = S("ones64f_s", [1, 64])
    g["sel2f"] = S("sel2f_s", [2, 128])
    g["ones128"] = S("ones128_s", [128, 128])
    g["iota"] = S("iota_s", [128, T], I16)

    OVR = 129 * 1024
    base, _end = nc.bump_sbuf(OVR)
    ovr_off = {}
    def at(name, shape, dtype=F32, o=None):
        nbytes = int(np.prod(shape[1:])) * mybir.dt.size(dtype)
        assert o is not None
        o = base + o
        assert o + nbytes <= base + OVR, f"overlay overflow at {name}: {o-base}+{nbytes}"
        ovr_off[name] = o - base
        return nc.alloc_sbuf_tensor_at(name, list(shape), dtype, offset=o)

    K = 1024
    g["Hq_fm"] = at("Hq_fm", [128, NKC, NT, 128], BF16, o=0)
    g["arena"] = at("arena", [128, NJ, BT, 512], BF16, o=32 * K)
    g["wbuf"] = at("wbuf", [128, 16, 512], BF16, o=64 * K)
    g["wdbuf"] = at("wdbuf", [128, 32, 256], BF16, o=64 * K)
    g["f32buf"] = at("f32buf", [128, NKC, 512], F32, o=80 * K)
    g["f32b2"] = at("f32b2", [128, NKC, 512], F32, o=64 * K)
    g["wbuf2"] = at("wbuf2", [128, 16, 512], BF16, o=80 * K)
    g["wdbuf2"] = at("wdbuf2", [128, 32, 256], BF16, o=80 * K)
    g["pqtmp"] = at("pqtmp", [128, BT, 512], BF16, o=96 * K)
    g["pqtmp2"] = at("pqtmp2", [128, BT, 512], BF16, o=124 * K + 256)
    g["hscr"] = at("hscr", [128, 2, C], F32, o=100 * K)
    g["i16s"] = at("i16s", [128, C], I16, o=108 * K)
    g["d16s"] = at("d16s", [128, C], BF16, o=110 * K)
    g["t16s"] = at("t16s", [128, C], BF16, o=112 * K)
    g["sqd"] = at("sqd", [128, C], BF16, o=114 * K)
    g["gtmp"] = at("gtmp", [128, 2, 512], BF16, o=116 * K)
    g["hqt"] = at("hqt", [128, 2, C], BF16, o=118 * K)
    g["i16b"] = at("i16b", [128, C], I16, o=122 * K)

    g["ru"] = at("ru", [128, T], F32, o=0)
    g["rk"] = at("rk", [128, T], I16, o=8 * K)
    g["rd"] = at("rd", [128, T], BF16, o=12 * K)
    g["rt"] = at("rt", [128, T], BF16, o=16 * K)
    g["rr"] = at("rr", [128, T], F32, o=20 * K)
    g["cosT"] = at("cosT", [128, T], F32, o=32 * K)
    g["sinT"] = at("sinT", [128, T], F32, o=40 * K)
    g["qkw"] = at("qkw", [128, NKC, 256], BF16, o=48 * K)
    g["vw"] = at("vw", [128, NKC, 128], BF16, o=52 * K)
    g["expS"] = at("expS", [128, 2, 512], BF16, o=54 * K)
    g["smr"] = at("smr", [2, 512], F32, o=56 * K)
    g["rn2"] = at("rn2", [2, 512], F32, o=58 * K)
    g["sqb"] = at("sqb", [128, 2, 512], BF16, o=60 * K)
    g["qksb"] = at("qksb", [128, 2, 512], BF16, o=62 * K)
    g["h2fm"] = at("h2fm", [128, NKC, NT, 128], BF16, o=80 * K)
    g["attn_fm"] = at("attn_fm", [128, NKC, NT, 128], BF16, o=0)
    g["wob"] = at("wob", [128, NKC, 512], BF16, o=112 * K)
    g["Qn"] = at("Qn", [128, T], BF16, o=112 * K)
    g["Kn"] = at("Kn", [128, T], BF16, o=116 * K)
    g["Vext"] = at("Vext", [128, NT, GH * 65], BF16, o=120 * K)
    g["h2stg"] = at("h2stg", [128, 2, C], BF16, o=124 * K + 256)
    g["ra"] = at("ra", [128, 2, 512], F32, o=64 * K)
    g["rb"] = at("rb", [128, 512], F32, o=56 * K + 512)
    g["pb"] = [st.enter_context(nc.psum_tensor(f"pb{i}", [128, 512], F32)) for i in range(8)]
    return nc, st, g


def emit_all(nc, st, g):
    P = Plan()
    s = {}
    for nm in ("x", "cst", "wf", "wf2", "twb", "trd", "trd2", "qw", "tp_hq", "tp_h2", "tp_at",
               "tp_pq", "te", "v", "a", "stm"):
        s[nm] = Sem(st, nc, "s_" + nm)
    pb = g["pb"]
    X, cvec, vec, vt, wsc = g["X"], g["cvec"], g["vec"], g["vt"], g["wsc"]
    VC = lambda b, ti: vec[:, b * NT + ti: b * NT + ti + 1]
    VTc = lambda i: vt[:, i:i + 1]
    CV = lambda i: cvec[:, i:i + 1]
    vsep = lambda e: e.tensor_copy(vt[:, 23:24], cvec[:, 0:1])

    xv = g["x_h"].rearrange("(i p) c -> p i c", p=128)
    P.op('sy', s["x"], 16, lambda e: e.dma_start(out=X[:], in_=xv))
    for dst, src in ((g["cvec"][:], g["cvec_h"]), (g["ident"][:], g["ident_h"]),
                     (g["pswap"][:], g["pswap_h"]),
                     (g["strips"][:], g["strips_h"].rearrange("s p f -> p s f")),
                     (g["blk"][:], g["blk_h"]), (g["blkw"][:], g["blkw_h"]),
                     (g["shift64"][:], g["shift64_h"]), (g["ones64f"][:], g["ones64f_h"]),
                     (g["sel2f"][:], g["sel2f_h"]),
                     (g["ones128"][:], g["ones128_h"]), (g["iota"][:], g["iota_h"])):
        P.op('sy', s["cst"], 16, lambda e, d=dst, sr=src: e.dma_start(out=d, in_=sr))

    def dbg_dump(name, src_ap, eng='sy'):
        if not DBG or name not in g:
            return
        P.wait(eng, s["v"])
        P.wait(eng, s["a"])
        P.wait(eng, s["te"])
        P.op(eng, s["stm"], 16, lambda e: e.dma_start(out=g[name], in_=src_ap))
        for k in ('sy', 'v', 'a', 't', 'gp'):
            P.wait(k, s["stm"])

    def init0(e):
        e.memset(vt[:], 0.0)
        e.memset(vec[:], 0.0)
        return e.memset(wsc[:], 0.0)
    P.op('v', s["v"], 1, init0)

    absum_guard = {"v": 0}

    def absum_units(src_h, col):
        nch = src_h.shape[0]
        per = int(np.prod(src_h.shape[2:]))
        nsub = (per + 4095) // 4096
        units = []
        state = {"acc": 0, "a_prev": [None, None]}
        idx = [0]
        for ch in range(nch):
            for sub in range(nsub):
                n = min(4096, per - sub * 4096)
                nq = n // C
                def unit(ch=ch, sub=sub, n=n, nq=nq):
                    i = idx[0]; idx[0] += 1
                    buf = g["f32buf"] if i % 2 == 0 else g["f32b2"]
                    wfs = s["wf"] if i % 2 == 0 else s["wf2"]
                    fb = buf[:].rearrange("p a b -> p (a b)")
                    if state["a_prev"][i % 2] is None:
                        P.wait('gp', s["a"])
                    else:
                        P.wait('gp', s["a"], state["a_prev"][i % 2])
                    P.op('gp', wfs, 16, lambda e, fb=fb: e.dma_start(
                        out=fb[:, :n],
                        in_=src_h[ch].rearrange("p a b -> p (a b)")[:, sub * 4096: sub * 4096 + n]))
                    P.wait('a', wfs)
                    P.wait('a', s["cst"])
                    P.wait('a', s["v"], absum_guard["v"])
                    base = 96 + state["acc"]
                    def absacc(e, nq=nq, base=base, fb=fb):
                        for q in range(nq):
                            inst = e.activation(out=g["sqd"][:], in_=fb[:, q * C:(q + 1) * C],
                                                func=AF.Abs, accum_out=vec[:, base + q: base + q + 1])
                        return inst
                    state["a_prev"][i % 2] = P.op('a', s["a"], 1, absacc)
                    state["acc"] += nq
                units.append(unit)
        def finalize():
            assert state["acc"] == 32
            P.wait('v', s["a"])
            absum_guard["v"] = P.op('v', s["v"], 1,
                lambda e: e.reduce_sum(VTc(14), vec[:, 96:128], axis=AX.X))
            P.wait('t', s["v"])
            P.wait('t', s["cst"])
            P.op('t', s["te"], 1, lambda e: e.matmul(pb[7][:, 0:1], g["ones128"][:], VTc(14),
                                                     start=True, stop=True))
            P.wait('v', s["te"])
            def fin(e, col=col):
                e.tensor_scalar(wsc[:, col:col + 1], pb[7][:, 0:1], INV_WN, 1e-5, ALU.mult, ALU.max)
                vsep(e)
                return e.reciprocal(wsc[:, col + 3:col + 4], wsc[:, col:col + 1])
            P.op('v', s["v"], 1, fin)
        return units, finalize

    def emit_weight_quant_chunk(src_h, ch, dst_sb, tern_h, col):
        f32buf = g["f32buf"]
        per = int(np.prod(src_h.shape[2:]))
        fb = f32buf[:].rearrange("p a b -> p (a b)")
        dst = dst_sb.rearrange("p a b -> p (a b)")
        nsub = (per + 4095) // 4096
        for sub in range(nsub):
            n = min(4096, per - sub * 4096)
            P.wait('gp', s["v"])
            P.op('gp', s["wf"], 16, lambda e, ch=ch, sub=sub, n=n: e.dma_start(
                out=fb[:, :n],
                in_=src_h[ch].rearrange("p a b -> p (a b)")[:, sub * 4096: sub * 4096 + n]))
            for q in range(n // C):
                sl = slice(q * C, (q + 1) * C)
                osl = slice(sub * 4096 + q * C, sub * 4096 + (q + 1) * C)
                P.wait('a', s["wf"])
                P.wait('a', s["v"])
                P.op('a', s["a"], 1, lambda e, sl=sl, col=col: e.activation(
                    out=g["i16s"][:], in_=fb[:, sl], func=AF.Copy,
                    scale=wsc[:, col + 3:col + 4]))
                P.wait('v', s["a"])
                def rnd(e, sl=sl, osl=osl, col=col):
                    e.scalar_tensor_tensor(g["d16s"][:], fb[:, sl], wsc[:, col + 3:col + 4],
                                           g["i16s"][:], ALU.mult, ALU.subtract)
                    e.scalar_tensor_tensor(g["t16s"][:], g["d16s"][:], 0.5, g["i16s"][:],
                                           ALU.is_gt, ALU.add)
                    e.scalar_tensor_tensor(g["d16s"][:], g["d16s"][:], -0.5, g["t16s"][:],
                                           ALU.is_lt, ALU.subtract)
                    return e.tensor_scalar(dst[:, osl], g["d16s"][:], -1.0, 1.0, ALU.max, ALU.min)
                P.op('v', s["v"], 1, rnd)
        P.wait('gp', s["v"])
        P.op('gp', s["twb"], 16, lambda e, ch=ch: e.dma_start(
            out=tern_h[ch].rearrange("p a b -> p (a b)"), in_=dst[:, :per]))

    ln_state = {"vt_free": [0, 0], "hb_free": [0, 0], "i16_free": [0, 0],
                "rnd_at": {}, "apply_at": {}}

    def emit_ln_pair(p0, quantize):
        st_ = ln_state
        xg = st_.get("x_guard", 0)
        tiles = [p0, p0 + 1]
        V = lambda n, it: vt[:, n + 8 * (it % 2): n + 8 * (it % 2) + 1]
        sq_at, stats_at, sqrt_at, rstd_at, qs_at, i16_at = {}, {}, {}, {}, {}, {}
        dump = g["sqd"][:] if quantize else g["rr"][:, 0:C]
        for it in tiles:
            P.wait('a', s["x"])
            P.wait('a', s["v"], max(xg, st_["vt_free"][it % 2]))
            sq_at[it] = P.op('a', s["a"], 1, lambda e, it=it: e.activation(
                out=dump, in_=X[:, it, :], func=AF.Square, accum_out=V(1, it)))
        for it in tiles:
            P.wait('v', s["a"], sq_at[it])
            def stats(e, it=it):
                e.reduce_sum(V(2, it), X[:, it, :], axis=AX.X)
                e.tensor_scalar(V(4, it), V(1, it), INV_C, None, ALU.mult)
                e.tensor_scalar(V(3, it), V(2, it), INV_C, None, ALU.mult)
                vsep(e)
                return e.scalar_tensor_tensor(V(5, it), V(3, it), V(3, it), V(4, it),
                                              ALU.mult, ALU.subtract)
            stats_at[it] = P.op('v', s["v"], 1, stats)
        for it in tiles:
            P.wait('a', s["v"], stats_at[it])
            sqrt_at[it] = P.op('a', s["a"], 1, lambda e, it=it: e.activation(
                out=V(6, it), in_=V(5, it), func=AF.Sqrt, scale=-1.0, bias=CV(7)))
        for it in tiles:
            P.wait('v', s["a"], sqrt_at[it])
            def rstd(e, it=it):
                e.reciprocal(V(7, it), V(6, it))
                vsep(e)
                return e.scalar_tensor_tensor(V(8, it), V(3, it), -1.0, V(7, it),
                                              ALU.mult, ALU.mult)
            rstd_at[it] = P.op('v', s["v"], 1, rstd)
        if not quantize:
            for it in tiles:
                h2t = g["h2stg"][:, it % 2, :]
                P.wait('a', s["v"], rstd_at[it])
                if it >= 2:
                    P.wait('a', s["tp_h2"], 16 * (it - 1))
                a_ap = P.op('a', s["a"], 1, lambda e, it=it, h2t=h2t: e.activation(
                    out=h2t, in_=X[:, it, :], func=AF.Identity, scale=V(7, it), bias=V(8, it)))
                P.wait('sy', s["a"], a_ap)
                P.wait('sy', s["tp_hq"])
                P.wait('sy', s["tp_pq"])
                P.wait('sy', s["tp_at"])
                P.op('sy', s["tp_h2"], 16, lambda e, it=it, h2t=h2t: e.dma_start_transpose(
                    g["h2fm"][:, :, it, :], h2t))
                st_["vt_free"][it % 2] = rstd_at[it]
            return
        for it in tiles:
            hb = g["hscr"][:, it % 2, :]
            P.wait('a', s["v"], rstd_at[it])
            P.wait('a', s["v"], st_["hb_free"][it % 2])
            st_["apply_at"][it] = P.op('a', s["a"], 1, lambda e, it=it, hb=hb: e.activation(
                out=hb, in_=X[:, it, :], func=AF.Identity, scale=V(7, it), bias=V(8, it)))
        for it in tiles:
            hb = g["hscr"][:, it % 2, :]
            P.wait('v', s["a"], st_["apply_at"][it])
            def qscale(e, it=it, hb=hb):
                e.reduce_max(V(1, it), hb, axis=AX.X, apply_absolute_value=True)
                vsep(e)
                e.tensor_scalar(VC(5, it), V(1, it), 1e-5, None, ALU.max)
                vsep(e)
                e.reciprocal(V(3, it), VC(5, it))
                vsep(e)
                return e.tensor_scalar(VC(3, it), V(3, it), 127.0, None, ALU.mult)
            qs_at[it] = P.op('v', s["v"], 1, qscale)
            st_["vt_free"][it % 2] = qs_at[it]
        for it in tiles:
            hb = g["hscr"][:, it % 2, :]
            i16 = g["i16s"] if it % 2 == 0 else g["i16b"]
            P.wait('a', s["v"], qs_at[it])
            P.wait('a', s["v"], st_["i16_free"][it % 2])
            i16_at[it] = P.op('a', s["a"], 1, lambda e, hb=hb, it=it, i16=i16: e.activation(
                out=i16[:], in_=hb, func=AF.Identity, scale=VC(3, it)))
        for it in tiles:
            hb = g["hscr"][:, it % 2, :]
            i16 = g["i16s"] if it % 2 == 0 else g["i16b"]
            hq = g["hqt"][:, it % 2, :]
            P.wait('v', s["a"], i16_at[it])
            if it >= 2:
                P.wait('v', s["tp_hq"], 16 * (it - 1))
            def rnd(e, hb=hb, it=it, hq=hq, i16=i16):
                e.scalar_tensor_tensor(g["d16s"][:], hb, VC(3, it), i16[:], ALU.mult, ALU.subtract)
                e.scalar_tensor_tensor(g["t16s"][:], g["d16s"][:], 0.5, i16[:], ALU.is_gt, ALU.add)
                return e.scalar_tensor_tensor(hq, g["d16s"][:], -0.5, g["t16s"][:], ALU.is_lt, ALU.subtract)
            n = P.op('v', s["v"], 1, rnd)
            ln_state["rnd_at"][it] = n
            st_["hb_free"][it % 2] = n
            st_["i16_free"][it % 2] = n
            P.wait('sy', s["v"], n)
            P.wait('sy', s["tp_pq"])
            P.wait('sy', s["tp_h2"])
            P.wait('sy', s["tp_at"])
            P.op('sy', s["tp_hq"], 16, lambda e, it=it, hq=hq: e.dma_start_transpose(
                g["Hq_fm"][:, :, it, :], hq))

    def emit_ln(quantize):
        ln_state["vt_free"] = [0, 0]
        ln_state["hb_free"] = [0, 0]
        ln_state["i16_free"] = [0, 0]
        ln_state["x_guard"] = s["v"].n
        for p0 in range(0, NT, 2):
            emit_ln_pair(p0, quantize)

    def emit_gu(stage, b):
        wbuf, gtmp = g["wbuf"], g["gtmp"]
        quant = (stage == 0 and b == 0)
        bank_free = [s["a"].n, s["a"].n, s["v"].n, s["v"].n]
        gt_free = [s["v"].n, s["v"].n]
        slot_te = [s["te"].n, s["te"].n]
        v_guard = s["v"].n
        for j in range(NJ):
            wb = wbuf if (quant or j % 2 == 0) else g["wbuf2"]
            gslot = wb[:, 0:NKC, :]
            uslot = wb[:, NKC:2 * NKC, :]
            if quant:
                if j > 0:
                    P.wait('v', s["te"])
                    P.wait('v', s["twb"])
                emit_weight_quant_chunk(g["wg_h"], j, gslot, g["tg_h"], 0)
                emit_weight_quant_chunk(g["wu_h"], j, uslot, g["tu_h"], 1)
                slot_rdy = ('v', s["v"].n, 0)
            else:
                trds = s["trd"] if j % 2 == 0 else s["trd2"]
                P.wait('gp', s["te"], slot_te[j % 2])
                if j < 2:
                    P.wait('gp', s["v"], v_guard)
                P.op('gp', trds, 16, lambda e, j=j, gslot=gslot: e.dma_start(
                    out=gslot.rearrange("p a b -> p (a b)"),
                    in_=g["tg_h"][j].rearrange("p a b -> p (a b)")))
                P.op('gp', trds, 16, lambda e, j=j, uslot=uslot: e.dma_start(
                    out=uslot.rearrange("p a b -> p (a b)"),
                    in_=g["tu_h"][j].rearrange("p a b -> p (a b)")))
                slot_rdy = ('trd', trds, trds.n)
            if slot_rdy[0] == 'v':
                P.wait('t', s["v"], slot_rdy[1])
            else:
                P.wait('t', slot_rdy[1], slot_rdy[2])
            P.wait('t', s["tp_hq"])
            for k in range(BT):
                ti = b * BT + k
                gbank, ubank = pb[k % 2], pb[2 + (k % 2)]
                P.wait('t', s["a"], bank_free[k % 2])
                def gmm(e, ti=ti, gbank=gbank, gslot=gslot):
                    for c in range(NKC):
                        inst = e.matmul(gbank[:], g["Hq_fm"][:, c, ti, :], gslot[:, c, :],
                                        start=(c == 0), stop=(c == NKC - 1))
                    return inst
                te_g = P.op('t', s["te"], 1, gmm)
                gt = gtmp[:, k % 2, :]
                P.wait('a', s["te"], te_g)
                P.wait('a', s["v"], gt_free[k % 2])
                bank_free[k % 2] = P.op('a', s["a"], 1,
                    lambda e, gt=gt, ti=ti, gbank=gbank: e.activation(
                        out=gt, in_=gbank[:], func=AF.Silu, scale=VC(0, ti)))
                P.wait('t', s["v"], bank_free[2 + k % 2])
                def umm(e, ti=ti, ubank=ubank, uslot=uslot):
                    for c in range(NKC):
                        inst = e.matmul(ubank[:], g["Hq_fm"][:, c, ti, :], uslot[:, c, :],
                                        start=(c == 0), stop=(c == NKC - 1))
                    return inst
                te_u = P.op('t', s["te"], 1, umm)
                P.wait('v', s["te"], te_u)
                P.wait('v', s["a"], bank_free[k % 2])
                n = P.op('v', s["v"], 1, lambda e, j=j, k=k, ti=ti, ubank=ubank, gt=gt:
                         e.scalar_tensor_tensor(g["arena"][:, j, k, :], ubank[:], VC(1, ti),
                                                gt, ALU.mult, ALU.mult))
                bank_free[2 + k % 2] = n
                gt_free[k % 2] = n
            slot_te[j % 2] = s["te"].n

    def emit_p(stage, b, tp_pq_prev):
        arena, pqtmp = g["arena"], g["pqtmp"]
        for k in range(BT):
            ti = b * BT + k
            P.wait('v', s["v"])
            def q2s(e, k=k, ti=ti):
                e.reduce_max(VTc(1), arena[:, :, k, :], axis=AX.XY, apply_absolute_value=True)
                vsep(e)
                e.tensor_scalar(VTc(2), VTc(1), 1e-5, None, ALU.max)
                vsep(e)
                e.reciprocal(VTc(3), VTc(2))
                e.scalar_tensor_tensor(VC(2, ti), VTc(2), 0.5 / 127.0, wsc[:, 2:3],
                                       ALU.mult, ALU.mult)
                return e.tensor_scalar(VC(4, ti), VTc(3), 127.0, None, ALU.mult)
            P.op('v', s["v"], 1, q2s)
        q2_done = s["v"].n
        seq = [(j, k) for j in range(NJ) for k in range(BT)]
        a_at, v_at = {}, {}
        def emit_cast(n):
            j, k = seq[n]
            ti = b * BT + k
            i16 = g["i16s"][:, (n % 2) * 512:(n % 2) * 512 + 512]
            P.wait('a', s["v"], max(q2_done, v_at.get(n - 2, 0)))
            a_at[n] = P.op('a', s["a"], 1, lambda e, j=j, k=k, ti=ti, i16=i16: e.activation(
                out=i16, in_=arena[:, j, k, :], func=AF.Identity, scale=VC(4, ti)))
        def emit_rnd(n):
            j, k = seq[n]
            ti = b * BT + k
            i16 = g["i16s"][:, (n % 2) * 512:(n % 2) * 512 + 512]
            pqt = g["pqtmp"] if j % 2 == 0 else g["pqtmp2"]
            P.wait('v', s["a"], a_at[n])
            if k == 0:
                P.wait('v', s["tp_pq"], tp_pq_prev[j % 2])
            def rnd(e, j=j, k=k, ti=ti, i16=i16, pqt=pqt):
                e.scalar_tensor_tensor(g["d16s"][:, 0:512], arena[:, j, k, :], VC(4, ti),
                                       i16, ALU.mult, ALU.subtract)
                e.scalar_tensor_tensor(g["t16s"][:, 0:512], g["d16s"][:, 0:512], 0.5,
                                       i16, ALU.is_gt, ALU.add)
                return e.scalar_tensor_tensor(pqt[:, k, :], g["d16s"][:, 0:512], -0.5,
                                              g["t16s"][:, 0:512], ALU.is_lt, ALU.subtract)
            v_at[n] = P.op('v', s["v"], 1, rnd)
            if k == BT - 1:
                P.wait('sy', s["v"], v_at[n])
                P.wait('sy', s["tp_pq"])
                P.wait('sy', s["tp_hq"])
                P.wait('sy', s["tp_h2"])
                P.wait('sy', s["tp_at"])
                tp_pq_prev[j % 2] = s["tp_pq"].n + 16
                P.op('sy', s["tp_pq"], 16, lambda e, j=j, pqt=pqt: e.dma_start_transpose(
                    arena[:, j].rearrange("p k (c q) -> p k c q", c=4), pqt[:]))
        for n in range(len(seq) + 1):
            if n < len(seq):
                emit_cast(n)
            if n >= 1:
                emit_rnd(n - 1)

    def emit_d(stage, b):
        wdbuf, arena = g["wdbuf"], g["arena"]
        quant = (stage == 0 and b == 0)
        bank_free = [s["v"].n, s["v"].n]
        dslot_te = [s["te"].n, s["te"].n]
        v_guard_d = s["v"].n
        for fc in range(NFD):
            wdb = wdbuf if (quant or fc % 2 == 0) else g["wdbuf2"]
            if quant:
                P.wait('v', s["te"])
                P.wait('v', s["twb"])
                emit_weight_quant_chunk(g["wd_h"], fc, wdb[:], g["td_h"], 2)
                P.wait('t', s["v"])
            else:
                trds = s["trd"] if fc % 2 == 0 else s["trd2"]
                P.wait('gp', s["te"], dslot_te[fc % 2])
                if fc < 2:
                    P.wait('gp', s["v"], v_guard_d)
                P.op('gp', trds, 16, lambda e, fc=fc, wdb=wdb: e.dma_start(
                    out=wdb[:].rearrange("p a b -> p (a b)"),
                    in_=g["td_h"][fc].rearrange("p a b -> p (a b)")))
                P.wait('t', trds)
            P.wait('t', s["tp_pq"])
            slotv = arena[:].rearrange("p j k (c q) -> p j k c q", c=4)
            for k in range(BT):
                ti = b * BT + k
                ybank = pb[4 + (k % 2)]
                P.wait('t', s["v"], bank_free[k % 2])
                def dmm(e, k=k, ybank=ybank, slotv=slotv, fc=fc, wdb=wdb):
                    for kc in range(32):
                        inst = e.matmul(ybank[:, 0:256], slotv[:, kc // 4, k, kc % 4, :],
                                        wdb[:, kc, :], start=(kc == 0), stop=(kc == 31))
                    return inst
                te_y = P.op('t', s["te"], 1, dmm)
                xsl = X[:, ti, fc * 256:(fc + 1) * 256]
                P.wait('v', s["te"], te_y)
                bank_free[k % 2] = P.op('v', s["v"], 1,
                    lambda e, ybank=ybank, ti=ti, xsl=xsl: e.scalar_tensor_tensor(
                        xsl, ybank[:, 0:256], VC(2, ti), xsl, ALU.mult, ALU.add))
            dslot_te[fc % 2] = s["te"].n

    def emit_ffn(stage):
        if stage == 0:
            units = []
            for srch, col in ((g["wg_h"], 0), (g["wu_h"], 1), (g["wd_h"], 2)):
                u, fin = absum_units(srch, col)
                units.append((u, fin))
            ln_state["vt_free"] = [0, 0]
            ln_state["hb_free"] = [0, 0]
            ln_state["i16_free"] = [0, 0]
            ln_state["x_guard"] = s["v"].n
            flat = [u for us, _ in units for u in us]
            fins = {len(units[0][0]) - 1: units[0][1],
                    len(units[0][0]) + len(units[1][0]) - 1: units[1][1],
                    len(units[0][0]) + len(units[1][0]) + len(units[2][0]) - 1: units[2][1]}
            pair_i = [0]
            for i, u in enumerate(flat):
                u()
                if i in fins:
                    fins[i]()
                if i % 3 == 2 and pair_i[0] < NT // 2:
                    emit_ln_pair(2 * pair_i[0], True)
                    pair_i[0] += 1
            while pair_i[0] < NT // 2:
                emit_ln_pair(2 * pair_i[0], True)
                pair_i[0] += 1
        else:
            emit_ln(quantize=True)
        def batch_scales(e):
            vsep(e)
            e.tensor_scalar(vec[:, 0:NT], vec[:, 5 * NT:6 * NT], wsc[:, 0:1], 1.0 / 127.0,
                            ALU.mult, ALU.mult)
            return e.tensor_scalar(vec[:, NT:2 * NT], vec[:, 5 * NT:6 * NT], wsc[:, 1:2],
                                   1.0 / 127.0, ALU.mult, ALU.mult)
        P.op('v', s["v"], 1, batch_scales)
        if CUT == 'absum' or CUT == 'ln':
            if DBG:
                dbg_dump("dbg_wsc", wsc[:])
                dbg_dump("dbg_hq", g["Hq_fm"][:])
                dbg_dump("dbg_vec", vec[:])
            return
        tp_pq_prev = [0, 0]
        for b in range(NB):
            emit_gu(stage, b)
            if stage == 0 and b == 0 and DBG:
                dbg_dump("dbg_wg", g["wbuf"][:])
                dbg_dump("dbg_arena", g["arena"][:])
            if CUT == 'gu0' and b == 0:
                return
            emit_p(stage, b, tp_pq_prev)
            if stage == 0 and b == 0 and DBG:
                dbg_dump("dbg_pq", g["arena"][:])
                dbg_dump("dbg_vec2", vec[:])
            if CUT == 'p0' and b == 0:
                return
            emit_d(stage, b)
            if stage == 0 and b == 0 and CUT == 'd0':
                return

    def emit_rope_tables():
        iota, cosT, sinT = g["iota"], g["cosT"], g["sinT"]
        ru, rk, rd, rt, rr = g["ru"], g["rk"], g["rd"], g["rt"], g["rr"]
        P.wait('v', s["cst"])
        def theta(e):
            e.tensor_scalar(ru[:], iota[:], CV(1), None, ALU.mult)
            return e.scalar_tensor_tensor(rr[:], iota[:], CV(0), ru[:], ALU.mult, ALU.add)
        P.op('v', s["v"], 1, theta)
        for phase, dst, signed in ((0.0, sinT, True), (float(np.pi / 2), cosT, False)):
            def reduce2(e, phase=phase):
                e.tensor_scalar(ru[:], rr[:], INV2PI, phase * INV2PI, ALU.mult, ALU.add)
                e.tensor_copy(rk[:], ru[:])
                e.tensor_tensor(rd[:], ru[:], rk[:], ALU.subtract)
                e.scalar_tensor_tensor(rt[:], rd[:], 0.5, rk[:], ALU.is_gt, ALU.add)
                e.scalar_tensor_tensor(rt[:], rd[:], -0.5, rt[:], ALU.is_lt, ALU.subtract)
                e.scalar_tensor_tensor(ru[:], rt[:], C_HI, rr[:], ALU.mult, ALU.add)
                e.tensor_scalar(ru[:], ru[:], 1.0, phase, ALU.mult, ALU.add)
                e.scalar_tensor_tensor(ru[:], rt[:], C_LO, ru[:], ALU.mult, ALU.add)
                return e.tensor_scalar(ru[:], ru[:], -PI_, PI_, ALU.max, ALU.min)
            P.op('v', s["v"], 1, reduce2)
            P.wait('a', s["v"])
            P.op('a', s["a"], 1, lambda e, dst=dst: e.activation(
                out=dst[:], in_=ru[:], func=AF.Sin))
            P.wait('v', s["a"])
            if signed:
                P.op('v', s["v"], 1, lambda e, dst=dst: e.tensor_scalar(
                    dst[:], dst[:], CV(2), None, ALU.mult))

    def emit_attn():
        av_evac = {}
        emit_rope_tables()
        emit_ln(quantize=False)
        strips = g["strips"]
        for grp in range(NG):
            P.wait('gp', s["te"])
            P.op('gp', s["qw"], 16, lambda e, grp=grp: e.dma_start(
                out=g["qkw"][:, :, 0:128], in_=g["wqkv_h"][grp]))
            P.op('gp', s["qw"], 16, lambda e, grp=grp: e.dma_start(
                out=g["qkw"][:, :, 128:256], in_=g["wqkv_h"][NG + grp]))
            qw_rdy = P.op('gp', s["qw"], 16, lambda e, grp=grp: e.dma_start(
                out=g["vw"][:], in_=g["wqkv_h"][2 * NG + grp]))
            iters = [(sel, tch) for sel in (0, 1) for tch in range(4)]
            qk_free = [s["v"].n, s["v"].n]
            st_free = [s["a"].n, s["a"].n]
            sw_free = [s["v"].n, s["v"].n]
            rn_free = [s["v"].n, s["v"].n]
            te_qk = {}
            a_evac = {}
            te_stats = {}
            a_rn = {}
            def emit_qk_mm(i):
                sel, tch = iters[i]
                qk = pb[i % 2]
                P.wait('t', s["qw"], qw_rdy)
                P.wait('t', s["tp_h2"])
                P.wait('t', s["v"], qk_free[i % 2])
                def qkmm(e, sel=sel, tch=tch, qk=qk):
                    rhs = g["h2fm"][:, :, 4 * tch:4 * tch + 4, :]
                    for c in range(NKC):
                        inst = e.matmul(qk[:], g["qkw"][:, c, sel * 128:(sel + 1) * 128],
                                        rhs[:, c].rearrange("p i q -> p (i q)"),
                                        start=(c == 0), stop=(c == NKC - 1))
                    return inst
                te_qk[i] = P.op('t', s["te"], 1, qkmm)
            def emit_evac(i):
                sel, tch = iters[i]
                qk = pb[i % 2]
                stb, swb, rnb = pb[2 + i % 2], pb[4 + i % 2], pb[6 + i % 2]
                sqb = g["sqb"][:, i % 2, :]
                qksb = g["qksb"][:, i % 2, :]
                P.wait('a', s["te"], te_qk[i])
                P.wait('a', s["te"], te_stats.get(i - 2, 0))
                P.wait('a', s["v"], av_evac.get("v", 0))
                P.op('a', s["a"], 1, lambda e, qk=qk, sqb=sqb: e.activation(
                    out=sqb, in_=qk[:], func=AF.Square))
                a_evac[i] = P.op('a', s["a"], 1, lambda e, qk=qk, qksb=qksb: e.activation(
                    out=qksb, in_=qk[:], func=AF.Copy))
                P.wait('t', s["a"], a_evac[i])
                P.wait('t', s["a"], st_free[i % 2])
                P.op('t', s["te"], 1, lambda e, stb=stb, sqb=sqb: e.matmul(
                    stb[:], g["blkw"][:], sqb, start=True, stop=True))
                P.wait('t', s["v"], sw_free[i % 2])
                te_stats[i] = P.op('t', s["te"], 1, lambda e, swb=swb, qksb=qksb: e.matmul(
                    swb[:], g["pswap"][:], qksb, start=True, stop=True))
                P.wait('a', s["te"], te_stats[i])
                P.wait('a', s["v"], rn_free[i % 2])
                P.op('a', s["a"], 1, lambda e, stb=stb, rnb=rnb: e.activation(
                    out=rnb[:], in_=stb[:], func=AF.Ln, scale=1.0 / 64.0, bias=CV(5)))
                st_free[i % 2] = s["a"].n
                a_rn[i] = P.op('a', s["a"], 1, lambda e, rnb=rnb: e.activation(
                    out=rnb[:], in_=rnb[:], func=AF.Exp, scale=-0.5))
                dstbuf = g["Qn"] if sel == 0 else g["Kn"]
                wcol = 3 if sel == 0 else 4
                tsl = slice(tch * 512, (tch + 1) * 512)
                ra = g["ra"][:, i % 2, :]
                P.wait('v', s["a"], a_rn[i])
                def rope(e, qk=qk, swb=swb, rnb=rnb, tsl=tsl, wcol=wcol, dstbuf=dstbuf, ra=ra):
                    e.tensor_tensor(ra, qk[:], g["cosT"][:, tsl], ALU.mult)
                    e.tensor_tensor(g["rb"][:], swb[:], g["sinT"][:, tsl], ALU.mult)
                    e.tensor_add(ra, ra, g["rb"][:])
                    return e.scalar_tensor_tensor(dstbuf[:, tsl], ra, CV(wcol), rnb[:],
                                                  ALU.mult, ALU.mult)
                n = P.op('v', s["v"], 1, rope)
                qk_free[i % 2] = n
                sw_free[i % 2] = n
                rn_free[i % 2] = n
            for i in range(len(iters) + 1):
                if i < len(iters):
                    emit_qk_mm(i)
                if i >= 1:
                    emit_evac(i - 1)
            for ti in range(NT):
                vbank = pb[ti % 2]
                P.wait('t', s["v"])
                def vmm(e, ti=ti, vbank=vbank):
                    for c in range(NKC):
                        inst = e.matmul(vbank[:, 0:128], g["h2fm"][:, c, ti, :],
                                        g["vw"][:, c, :], start=(c == 0), stop=(c == NKC - 1))
                    return inst
                te_v = P.op('t', s["te"], 1, vmm)
                P.wait('v', s["te"], te_v)
                def vevac(e, ti=ti, vbank=vbank):
                    e.tensor_copy(g["Vext"][:, ti, :].rearrange("p (h d) -> p h d", d=65)[:, :, 0:64],
                                  vbank[:, 0:128].rearrange("p (h d) -> p h d", d=64))
                    return e.memset(g["Vext"][:, ti, 64::65], 1.0)
                P.op('v', s["v"], 1, vevac)
            chunk_list = [(e_i, tch) for e_i in range(2) for tch in range(4)]
            av_last = {}
            def emit_sav(ci):
                e_i, tch = chunk_list[ci]
                ci_ = ci
                ntk = 4 * (tch + 1)
                qsl = slice(tch * 512, (tch + 1) * 512)
                avb = pb[2 + ci % 2]
                exp_at = {}
                av_at = {}
                def emit_s(tkt):
                    diag = tkt - 4 * tch
                    sb = pb[tkt % 2]
                    if tkt >= 2:
                        P.wait('t', s["a"], exp_at[tkt - 2])
                    ksl = slice(tkt * 128, (tkt + 1) * 128)
                    if tkt < 2 and ci > 0:
                        pass
                    def smm(e, e_i=e_i, ksl=ksl, qsl=qsl, sb=sb, diag=diag):
                        i1 = e.matmul(sb[:], g["Kn"][64 * e_i:64 * (e_i + 1), ksl],
                                      g["Qn"][64 * e_i:64 * (e_i + 1), qsl],
                                      start=True, stop=(diag < 0))
                        if diag >= 0:
                            i1 = e.matmul(sb[:], g["ident"][:], strips[:, diag, :],
                                          start=False, stop=True)
                        return i1
                    te_s = P.op('t', s["te"], 1, smm)
                    P.wait('a', s["te"], te_s)
                    if tkt >= 2:
                        P.wait('a', s["te"], av_at[tkt - 2])
                    elif ci > 0:
                        P.wait('a', s["te"], av_last[ci - 1])
                    P.op('a', s["a"], 1, lambda e, tkt=tkt, sb=sb: e.activation(
                        out=g["expS"][:, tkt % 2, :], in_=sb[:], func=AF.Exp,
                        scale=0.125, bias=CV(6)))
                    exp_at[tkt] = s["a"].n
                def emit_av(tkt):
                    P.wait('t', s["a"], exp_at[tkt])
                    if tkt == 0:
                        P.wait('t', s["v"], av_evac.get(("av", ci % 2), 0))
                    def avmm(e, e_i=e_i, tkt=tkt, ntk=ntk, avb=avb):
                        return e.matmul(avb[0:65, :],
                                        g["Vext"][:, tkt, e_i * 65:(e_i + 1) * 65],
                                        g["expS"][:, tkt % 2, :],
                                        start=(tkt == 0), stop=(tkt == ntk - 1))
                    av_at[tkt] = P.op('t', s["te"], 1, avmm)
                for tkt in range(ntk + 1):
                    if tkt < ntk:
                        emit_s(tkt)
                    if tkt >= 1:
                        emit_av(tkt - 1)
                av_last[ci] = av_at[ntk - 1]
            def emit_evac(ci):
                e_i, tch = chunk_list[ci]
                head = 2 * grp + e_i
                avb = pb[2 + ci % 2]
                lsb = g["rb"][0:1, :]
                P.wait('a', s["te"], av_last[ci])
                P.wait('a', s["v"], av_evac.get("lsb", 0))
                P.op('a', s["a"], 1, lambda e, avb=avb: e.activation(
                    out=lsb, in_=avb[64:65, :], func=AF.Ln))
                a_ln = P.op('a', s["a"], 1, lambda e: e.activation(
                    out=lsb, in_=lsb, func=AF.Exp, scale=-1.0))
                lb_i = 6 + ci % 2
                lbb = pb[lb_i]
                P.wait('t', s["a"], a_ln)
                P.wait('t', s["v"], av_evac.get(("lb", lb_i), 0))
                te_lb = P.op('t', s["te"], 1, lambda e, lbb=lbb: e.matmul(
                    lbb[0:64, :], g["ones64f"][:], lsb, start=True, stop=True))
                av_evac["lsb"] = s["v"].n
                P.wait('v', s["te"], te_lb)
                linvb = g["ra"][0:64, ci % 2, :]
                P.op('v', s["v"], 1, lambda e, lbb=lbb, linvb=linvb: e.tensor_copy(linvb, lbb[0:64, :]))
                av_evac[("lb", lb_i)] = s["v"].n
                av_evac["lsb"] = s["v"].n
                dst = g["attn_fm"][(head % 2) * 64:(head % 2) * 64 + 64, head // 2,
                                   4 * tch:4 * tch + 4, :]
                if head % 2 == 0:
                    n = P.op('v', s["v"], 1, lambda e, avb=avb, linvb=linvb, dst=dst: e.tensor_tensor(
                        dst.rearrange("p i q -> p (i q)"), avb[0:64, :], linvb, ALU.mult))
                    av_evac[("av", ci % 2)] = n
                    av_evac["v"] = n
                else:
                    nsb = g["qksb"][:, ci % 2, :][0:64, :]
                    P.op('v', s["v"], 1, lambda e, avb=avb, linvb=linvb, nsb=nsb: e.tensor_tensor(
                        nsb, avb[0:64, :], linvb, ALU.mult))
                    v_ns = s["v"].n
                    av_evac[("av", ci % 2)] = v_ns
                    prm_i = 6 + (ci + 1) % 2
                    prm = pb[prm_i]
                    P.wait('t', s["v"], max(v_ns, av_evac.get(("lb", prm_i), 0)))
                    te_pm = P.op('t', s["te"], 1, lambda e, nsb=nsb, prm=prm: e.matmul(
                        prm[:], g["shift64"][:], nsb, start=True, stop=True))
                    P.wait('v', s["te"], te_pm)
                    n = P.op('v', s["v"], 1, lambda e, prm=prm, dst=dst: e.tensor_copy(
                        dst.rearrange("p i q -> p (i q)"), prm[64:128, :]))
                    av_evac["v"] = n
                    av_evac[("lb", prm_i)] = n
            for ci in range(len(chunk_list) + 1):
                if ci < len(chunk_list):
                    emit_sav(ci)
                if ci >= 1:
                    emit_evac(ci - 1)
        bank_free = [s["v"].n, s["v"].n]
        for fc in range(2):
            P.wait('gp', s["te"])
            qw_rdy = P.op('gp', s["qw"], 16, lambda e, fc=fc: e.dma_start(
                out=g["wob"][:], in_=g["wo_h"][fc]))
            for it in range(NT):
                yb = pb[2 + (it % 2)]
                P.wait('t', s["qw"], qw_rdy)
                P.wait('t', s["tp_at"])
                P.wait('t', s["v"], bank_free[it % 2])
                def omm(e, it=it, yb=yb):
                    for c in range(NKC):
                        inst = e.matmul(yb[:], g["attn_fm"][:, c, it, :], g["wob"][:, c, :],
                                        start=(c == 0), stop=(c == NKC - 1))
                    return inst
                te_y = P.op('t', s["te"], 1, omm)
                xsl = X[:, it, fc * 512:(fc + 1) * 512]
                P.wait('v', s["te"], te_y)
                bank_free[it % 2] = P.op('v', s["v"], 1,
                    lambda e, yb=yb, xsl=xsl: e.tensor_add(xsl, xsl, yb[:]))

    if DO_A:
        emit_ffn(0)
    if DO_B:
        emit_attn()
    if DO_C:
        emit_ffn(2)
    ov = g["out_h"].rearrange("(i p) c -> p i c", p=128)
    P.wait('sy', s["v"])
    P.wait('sy', s["a"])
    P.wait('sy', s["te"])
    P.op('sy', s["stm"], 16, lambda e: e.dma_start(out=ov, in_=X[:]))
    P.wait('sy', s["stm"])

    with nc.Block() as block:
        _fill_block(block, P)
    return P


def _fill_block(block, P):
    @block.sync
    def _(e):
        P.run('sy', e)
    @block.gpsimd
    def _(e):
        P.run('gp', e)
    @block.vector
    def _(e):
        P.run('v', e)
    @block.scalar
    def _(e):
        P.run('a', e)
    @block.tensor
    def _(e):
        P.run('t', e)


_NC_CACHE = {}

def _get_nc():
    key = (DO_A, DO_B, DO_C, CUT, DBG)
    if key not in _NC_CACHE:
        nc, st, g = build_program()
        emit_all(nc, st, g)
        _NC_CACHE[key] = (nc, st, g)
    return _NC_CACHE[key]


def kernel(L0_attn_x, L0_attn_w_qkv, L0_attn_w_out, L0_attn_q_norm_w,
           L0_attn_k_norm_w, L0_ffn_w_gate, L0_ffn_w_up, L0_ffn_w_down):
    consts = host_prepare(L0_attn_w_qkv, L0_attn_w_out, L0_attn_q_norm_w,
                          L0_attn_k_norm_w, L0_ffn_w_gate, L0_ffn_w_up, L0_ffn_w_down)
    x = np.ascontiguousarray(np.asarray(L0_attn_x, np.float32))
    nc, _st, _g = _get_nc()
    in_maps = []
    for b in range(8):
        m = {"x": np.ascontiguousarray(x[b])}
        m.update(consts)
        in_maps.append(m)
    res = run_bass_kernel_spmd(nc, in_maps, core_ids=list(range(8)))
    out = np.stack([np.asarray(r["out"]) for r in res.results], axis=0)
    return out.astype(np.float32)


# version 06
# speedup vs baseline: 1.1965x; 1.0057x over previous
import numpy as np
import ml_dtypes
from contextlib import ExitStack

import concourse.bass as bass
import concourse.mybir as mybir
from concourse.bass_utils import run_bass_kernel_spmd

F32 = mybir.dt.float32
BF16 = mybir.dt.bfloat16
I16 = mybir.dt.int16
ALU = mybir.AluOpType
AF = mybir.ActivationFunctionType
AX = mybir.AxisListType

T, C, H, D, INNER = 2048, 1024, 16, 64, 4096
NT = T // 128
NB = 4
BT = NT // NB
NJ = INNER // 512
NKC = C // 128
NFD = 4
GH = 2
NG = H // GH

EPS_LN, EPS_RMS = 1e-5, 1e-6
MSK = -2000.0
INV_C = 1.0 / C
INV_WN = 1.0 / (1024 * 4096)
TWO_PI = 6.283185307179586
C_HI = float(np.float32(TWO_PI))
C_LO = float(np.float64(TWO_PI) - np.float64(np.float32(TWO_PI)))
INV2PI = 1.0 / TWO_PI
PI_ = float(np.pi)

DO_A, DO_B, DO_C = True, True, True
DBG = False
CUT = None


class Sem:
    def __init__(self, st, nc, name):
        self.h = st.enter_context(nc.semaphore(name))
        self.n = 0


class Plan:
    def __init__(self):
        self.ops = []
    def on(self, eng, fn):
        self.ops.append((eng, fn))
    def op(self, eng, sem, k, fn):
        self.ops.append((eng, lambda e, h=sem.h, k=k, fn=fn: fn(e).then_inc(h, k)))
        sem.n += k
        return sem.n
    def wait(self, eng, sem, val=None):
        v = sem.n if val is None else val
        if v > 0:
            self.ops.append((eng, lambda e, h=sem.h, v=v: e.wait_ge(h, v)))
    def run(self, key, e):
        for k, fn in self.ops:
            if k == key:
                fn(e)


def host_prepare(w_qkv, w_out, q_norm_w, k_norm_w, w_gate, w_up, w_down):
    f32 = np.float32
    out = {}
    wgT = np.asarray(w_gate, f32).T
    wuT = np.asarray(w_up, f32).T
    wdT = np.asarray(w_down, f32).T
    for nm, w in (("wg", wgT), ("wu", wuT)):
        out[nm] = np.ascontiguousarray(w.reshape(NKC, 128, NJ, 512).transpose(2, 1, 0, 3))
    out["wd"] = np.ascontiguousarray(wdT.reshape(32, 128, NFD, 256).transpose(2, 1, 0, 3))
    wqkvT = np.asarray(w_qkv, f32).T
    out["wqkv"] = np.ascontiguousarray(wqkvT.reshape(NKC, 128, 24, 128).transpose(2, 1, 0, 3))
    woT = np.asarray(w_out, f32).T
    out["wo"] = np.ascontiguousarray(woT.reshape(NKC, 128, 2, 512).transpose(2, 1, 0, 3))

    p = np.arange(128)
    out["iota_t"] = np.broadcast_to(np.arange(T, dtype=np.int16), (128, T)).copy()
    w64 = 10000.0 ** (-(p % 32).astype(np.float64) / 32.0)
    whi = w64.astype(f32)
    wlo = (w64 - whi.astype(np.float64)).astype(f32)
    cvec = np.stack([
        whi, wlo,
        np.where((p % 64) < 32, -1.0, 1.0).astype(f32),
        np.asarray(q_norm_w, f32)[p % 64],
        np.asarray(k_norm_w, f32)[p % 64],
        np.full(128, EPS_RMS, f32),
        np.full(128, -8.0, f32),
        np.full(128, EPS_LN, f32),
    ], axis=1)
    out["cvec"] = np.ascontiguousarray(cvec)
    out["ident"] = np.eye(128, dtype=f32).astype(ml_dtypes.bfloat16)
    strips = np.zeros((4, 128, 512), f32)
    ff = np.arange(512)[None, :]
    for sgm in range(4):
        strips[sgm] = np.where(p[:, None] + 128 * sgm > ff, MSK, 0.0)
    out["strips"] = strips.astype(ml_dtypes.bfloat16)
    blk = np.zeros((128, 2), f32); blk[:64, 0] = 1; blk[64:, 1] = 1
    bw = np.zeros((128, 128), f32)
    bw[:64, :64] = 1; bw[64:, 64:] = 1
    out["blkw"] = bw.astype(ml_dtypes.bfloat16)
    out["blk"] = blk.astype(ml_dtypes.bfloat16)
    out["sel2f"] = np.ascontiguousarray(blk.T)
    out["ones128"] = np.ones((128, 128), f32)
    sw = (p // 64) * 64 + ((p + 32) % 64)
    pm = np.zeros((128, 128), f32)
    pm[sw, p] = 1.0
    out["pswap"] = pm.astype(ml_dtypes.bfloat16)
    sh = np.zeros((64, 128), f32)
    sh[np.arange(64), np.arange(64) + 64] = 1.0
    out["shift64"] = sh.astype(ml_dtypes.bfloat16)
    out["ones64f"] = np.ones((1, 64), f32)
    return out


def build_program():
    nc = bass.Bass("TRN2", target_bir_lowering=False, debug=False, detect_race_conditions=False)
    st = ExitStack()
    g = {}
    din = lambda n, sh, dt=F32: nc.dram_tensor(n, sh, dt, kind="ExternalInput").ap()
    g["x_h"] = din("x", [T, C])
    g["wg_h"] = din("wg", [NJ, 128, NKC, 512])
    g["wu_h"] = din("wu", [NJ, 128, NKC, 512])
    g["wd_h"] = din("wd", [NFD, 128, 32, 256])
    g["wqkv_h"] = din("wqkv", [24, 128, NKC, 128])
    g["wo_h"] = din("wo", [2, 128, NKC, 512])
    g["iota_h"] = din("iota_t", [128, T], I16)
    g["cvec_h"] = din("cvec", [128, 8])
    g["ident_h"] = din("ident", [128, 128], BF16)
    g["pswap_h"] = din("pswap", [128, 128], BF16)
    g["strips_h"] = din("strips", [4, 128, 512], BF16)
    g["blk_h"] = din("blk", [128, 2], BF16)
    g["blkw_h"] = din("blkw", [128, 128], BF16)
    g["shift64_h"] = din("shift64", [64, 128], BF16)
    g["ones64f_h"] = din("ones64f", [1, 64])
    g["sel2f_h"] = din("sel2f", [2, 128])
    g["ones128_h"] = din("ones128", [128, 128])
    g["out_h"] = nc.dram_tensor("out", [T, C], F32, kind="ExternalOutput").ap()
    g["tg_h"] = nc.dram_tensor("tern_g", [NJ, 128, NKC, 512], BF16, kind="Internal").ap()
    g["tu_h"] = nc.dram_tensor("tern_u", [NJ, 128, NKC, 512], BF16, kind="Internal").ap()
    g["td_h"] = nc.dram_tensor("tern_d", [NFD, 128, 32, 256], BF16, kind="Internal").ap()
    if DBG:
        g["dbg_hq"] = nc.dram_tensor("dbg_hq", [128, NKC, NT, 128], BF16, kind="ExternalOutput").ap()
        g["dbg_vec"] = nc.dram_tensor("dbg_vec", [128, 8 * NT], F32, kind="ExternalOutput").ap()
        g["dbg_h"] = nc.dram_tensor("dbg_h", [128, C], F32, kind="ExternalOutput").ap()
        g["dbg_vec2"] = nc.dram_tensor("dbg_vec2", [128, 8 * NT], F32, kind="ExternalOutput").ap()
        g["dbg_vt"] = nc.dram_tensor("dbg_vt", [128, 16], F32, kind="ExternalOutput").ap()
        g["dbg_wsc"] = nc.dram_tensor("dbg_wsc", [128, 8], F32, kind="ExternalOutput").ap()
        g["dbg_wg"] = nc.dram_tensor("dbg_wg", [128, 16, 512], BF16, kind="ExternalOutput").ap()
        g["dbg_arena"] = nc.dram_tensor("dbg_arena", [128, NJ, BT, 512], BF16, kind="ExternalOutput").ap()
        g["dbg_pq"] = nc.dram_tensor("dbg_pq", [128, NJ, BT, 512], BF16, kind="ExternalOutput").ap()

    S = lambda n, sh, dt=F32: st.enter_context(nc.sbuf_tensor(n, sh, dt))
    g["X"] = S("X", [128, NT, C])
    g["cvec"] = S("cvec_s", [128, 8])
    g["vec"] = S("vec", [128, 8 * NT])
    g["vt"] = S("vt", [128, 24])
    g["wsc"] = S("wsc", [128, 8])
    g["ident"] = S("ident_s", [128, 128], BF16)
    g["pswap"] = S("pswap_s", [128, 128], BF16)
    g["strips"] = S("strips_s", [128, 4, 512], BF16)
    g["blk"] = S("blk_s", [128, 2], BF16)
    g["blkw"] = S("blkw_s", [128, 128], BF16)
    g["shift64"] = S("shift64_s", [64, 128], BF16)
    g["ones64f"] = S("ones64f_s", [1, 64])
    g["sel2f"] = S("sel2f_s", [2, 128])
    g["ones128"] = S("ones128_s", [128, 128])
    g["iota"] = S("iota_s", [128, T], I16)

    OVR = 129 * 1024
    base, _end = nc.bump_sbuf(OVR)
    ovr_off = {}
    def at(name, shape, dtype=F32, o=None):
        nbytes = int(np.prod(shape[1:])) * mybir.dt.size(dtype)
        assert o is not None
        o = base + o
        assert o + nbytes <= base + OVR, f"overlay overflow at {name}: {o-base}+{nbytes}"
        ovr_off[name] = o - base
        return nc.alloc_sbuf_tensor_at(name, list(shape), dtype, offset=o)

    K = 1024
    g["Hq_fm"] = at("Hq_fm", [128, NKC, NT, 128], BF16, o=0)
    g["arena"] = at("arena", [128, NJ, BT, 512], BF16, o=32 * K)
    g["wbuf"] = at("wbuf", [128, 16, 512], BF16, o=64 * K)
    g["wdbuf"] = at("wdbuf", [128, 32, 256], BF16, o=64 * K)
    g["f32buf"] = at("f32buf", [128, NKC, 512], F32, o=80 * K)
    g["f32b2"] = at("f32b2", [128, NKC, 512], F32, o=64 * K)
    g["wbuf2"] = at("wbuf2", [128, 16, 512], BF16, o=80 * K)
    g["wdbuf2"] = at("wdbuf2", [128, 32, 256], BF16, o=80 * K)
    g["pqtmp"] = at("pqtmp", [128, BT, 512], BF16, o=96 * K)
    g["pqtmp2"] = at("pqtmp2", [128, BT, 512], BF16, o=124 * K + 256)
    g["hscr"] = at("hscr", [128, 2, C], F32, o=100 * K)
    g["i16s"] = at("i16s", [128, C], I16, o=108 * K)
    g["d16s"] = at("d16s", [128, C], BF16, o=110 * K)
    g["t16s"] = at("t16s", [128, C], BF16, o=112 * K)
    g["sqd"] = at("sqd", [128, C], BF16, o=114 * K)
    g["gtmp"] = at("gtmp", [128, 2, 512], BF16, o=116 * K)
    g["hqt"] = at("hqt", [128, 2, C], BF16, o=118 * K)
    g["i16b"] = at("i16b", [128, C], I16, o=122 * K)

    g["ru"] = at("ru", [128, T], F32, o=0)
    g["rk"] = at("rk", [128, T], I16, o=8 * K)
    g["rd"] = at("rd", [128, T], BF16, o=12 * K)
    g["rt"] = at("rt", [128, T], BF16, o=16 * K)
    g["rr"] = at("rr", [128, T], F32, o=20 * K)
    g["cosT"] = at("cosT", [128, T], F32, o=32 * K)
    g["sinT"] = at("sinT", [128, T], F32, o=40 * K)
    g["qkw"] = at("qkw", [128, NKC, 256], BF16, o=48 * K)
    g["vw"] = at("vw", [128, NKC, 128], BF16, o=52 * K)
    g["expS"] = at("expS", [128, 2, 512], BF16, o=54 * K)
    g["smr"] = at("smr", [2, 512], F32, o=56 * K)
    g["rn2"] = at("rn2", [2, 512], F32, o=58 * K)
    g["sqb"] = at("sqb", [128, 2, 512], BF16, o=60 * K)
    g["qksb"] = at("qksb", [128, 2, 512], BF16, o=62 * K)
    g["h2fm"] = at("h2fm", [128, NKC, NT, 128], BF16, o=80 * K)
    g["attn_fm"] = at("attn_fm", [128, NKC, NT, 128], BF16, o=0)
    g["wob"] = at("wob", [128, NKC, 512], BF16, o=112 * K)
    g["Qn"] = at("Qn", [128, T], BF16, o=112 * K)
    g["Kn"] = at("Kn", [128, T], BF16, o=116 * K)
    g["Vext"] = at("Vext", [128, NT, GH * 65], BF16, o=120 * K)
    g["h2stg"] = at("h2stg", [128, 2, C], BF16, o=124 * K + 256)
    g["ra"] = at("ra", [128, 2, 512], F32, o=64 * K)
    g["qkw2"] = at("qkw2", [128, NKC, 256], BF16, o=68 * K)
    g["vw2"] = at("vw2", [128, NKC, 128], BF16, o=72 * K)
    g["rb"] = at("rb", [128, 512], F32, o=56 * K + 512)
    g["pb"] = [st.enter_context(nc.psum_tensor(f"pb{i}", [128, 512], F32)) for i in range(8)]
    return nc, st, g


def emit_all(nc, st, g):
    P = Plan()
    s = {}
    for nm in ("x", "cst", "wf", "wf2", "twb", "trd", "trd2", "qw", "tp_hq", "tp_h2", "tp_at",
               "tp_pq", "te", "v", "a", "stm"):
        s[nm] = Sem(st, nc, "s_" + nm)
    pb = g["pb"]
    X, cvec, vec, vt, wsc = g["X"], g["cvec"], g["vec"], g["vt"], g["wsc"]
    VC = lambda b, ti: vec[:, b * NT + ti: b * NT + ti + 1]
    VTc = lambda i: vt[:, i:i + 1]
    CV = lambda i: cvec[:, i:i + 1]
    vsep = lambda e: e.tensor_copy(vt[:, 23:24], cvec[:, 0:1])

    xv = g["x_h"].rearrange("(i p) c -> p i c", p=128)
    P.op('sy', s["x"], 16, lambda e: e.dma_start(out=X[:], in_=xv))
    for dst, src in ((g["cvec"][:], g["cvec_h"]), (g["ident"][:], g["ident_h"]),
                     (g["pswap"][:], g["pswap_h"]),
                     (g["strips"][:], g["strips_h"].rearrange("s p f -> p s f")),
                     (g["blk"][:], g["blk_h"]), (g["blkw"][:], g["blkw_h"]),
                     (g["shift64"][:], g["shift64_h"]), (g["ones64f"][:], g["ones64f_h"]),
                     (g["sel2f"][:], g["sel2f_h"]),
                     (g["ones128"][:], g["ones128_h"]), (g["iota"][:], g["iota_h"])):
        P.op('sy', s["cst"], 16, lambda e, d=dst, sr=src: e.dma_start(out=d, in_=sr))

    def dbg_dump(name, src_ap, eng='sy'):
        if not DBG or name not in g:
            return
        P.wait(eng, s["v"])
        P.wait(eng, s["a"])
        P.wait(eng, s["te"])
        P.op(eng, s["stm"], 16, lambda e: e.dma_start(out=g[name], in_=src_ap))
        for k in ('sy', 'v', 'a', 't', 'gp'):
            P.wait(k, s["stm"])

    def init0(e):
        e.memset(vt[:], 0.0)
        e.memset(vec[:], 0.0)
        return e.memset(wsc[:], 0.0)
    P.op('v', s["v"], 1, init0)

    absum_guard = {"v": 0}

    def absum_units(src_h, col):
        nch = src_h.shape[0]
        per = int(np.prod(src_h.shape[2:]))
        nsub = (per + 4095) // 4096
        units = []
        state = {"acc": 0, "a_prev": [None, None]}
        idx = [0]
        for ch in range(nch):
            for sub in range(nsub):
                n = min(4096, per - sub * 4096)
                nq = n // C
                def unit(ch=ch, sub=sub, n=n, nq=nq):
                    i = idx[0]; idx[0] += 1
                    buf = g["f32buf"] if i % 2 == 0 else g["f32b2"]
                    wfs = s["wf"] if i % 2 == 0 else s["wf2"]
                    fb = buf[:].rearrange("p a b -> p (a b)")
                    if state["a_prev"][i % 2] is None:
                        P.wait('gp', s["a"])
                    else:
                        P.wait('gp', s["a"], state["a_prev"][i % 2])
                    P.op('gp', wfs, 16, lambda e, fb=fb: e.dma_start(
                        out=fb[:, :n],
                        in_=src_h[ch].rearrange("p a b -> p (a b)")[:, sub * 4096: sub * 4096 + n]))
                    P.wait('a', wfs)
                    P.wait('a', s["cst"])
                    P.wait('a', s["v"], absum_guard["v"])
                    base = 96 + state["acc"]
                    def absacc(e, nq=nq, base=base, fb=fb):
                        for q in range(nq):
                            inst = e.activation(out=g["sqd"][:], in_=fb[:, q * C:(q + 1) * C],
                                                func=AF.Abs, accum_out=vec[:, base + q: base + q + 1])
                        return inst
                    state["a_prev"][i % 2] = P.op('a', s["a"], 1, absacc)
                    state["acc"] += nq
                units.append(unit)
        def finalize():
            assert state["acc"] == 32
            P.wait('v', s["a"])
            absum_guard["v"] = P.op('v', s["v"], 1,
                lambda e: e.reduce_sum(VTc(14), vec[:, 96:128], axis=AX.X))
            P.wait('t', s["v"])
            P.wait('t', s["cst"])
            P.op('t', s["te"], 1, lambda e: e.matmul(pb[7][:, 0:1], g["ones128"][:], VTc(14),
                                                     start=True, stop=True))
            P.wait('v', s["te"])
            def fin(e, col=col):
                e.tensor_scalar(wsc[:, col:col + 1], pb[7][:, 0:1], INV_WN, 1e-5, ALU.mult, ALU.max)
                vsep(e)
                return e.reciprocal(wsc[:, col + 3:col + 4], wsc[:, col:col + 1])
            P.op('v', s["v"], 1, fin)
        return units, finalize

    def emit_weight_quant_chunk(src_h, ch, dst_sb, tern_h, col):
        f32buf = g["f32buf"]
        per = int(np.prod(src_h.shape[2:]))
        fb = f32buf[:].rearrange("p a b -> p (a b)")
        dst = dst_sb.rearrange("p a b -> p (a b)")
        nsub = (per + 4095) // 4096
        for sub in range(nsub):
            n = min(4096, per - sub * 4096)
            P.wait('gp', s["v"])
            P.op('gp', s["wf"], 16, lambda e, ch=ch, sub=sub, n=n: e.dma_start(
                out=fb[:, :n],
                in_=src_h[ch].rearrange("p a b -> p (a b)")[:, sub * 4096: sub * 4096 + n]))
            for q in range(n // C):
                sl = slice(q * C, (q + 1) * C)
                osl = slice(sub * 4096 + q * C, sub * 4096 + (q + 1) * C)
                P.wait('a', s["wf"])
                P.wait('a', s["v"])
                P.op('a', s["a"], 1, lambda e, sl=sl, col=col: e.activation(
                    out=g["i16s"][:], in_=fb[:, sl], func=AF.Copy,
                    scale=wsc[:, col + 3:col + 4]))
                P.wait('v', s["a"])
                def rnd(e, sl=sl, osl=osl, col=col):
                    e.scalar_tensor_tensor(g["d16s"][:], fb[:, sl], wsc[:, col + 3:col + 4],
                                           g["i16s"][:], ALU.mult, ALU.subtract)
                    e.scalar_tensor_tensor(g["t16s"][:], g["d16s"][:], 0.5, g["i16s"][:],
                                           ALU.is_gt, ALU.add)
                    e.scalar_tensor_tensor(g["d16s"][:], g["d16s"][:], -0.5, g["t16s"][:],
                                           ALU.is_lt, ALU.subtract)
                    return e.tensor_scalar(dst[:, osl], g["d16s"][:], -1.0, 1.0, ALU.max, ALU.min)
                P.op('v', s["v"], 1, rnd)
        P.wait('gp', s["v"])
        P.op('gp', s["twb"], 16, lambda e, ch=ch: e.dma_start(
            out=tern_h[ch].rearrange("p a b -> p (a b)"), in_=dst[:, :per]))

    ln_state = {"vt_free": [0, 0], "hb_free": [0, 0], "i16_free": [0, 0],
                "rnd_at": {}, "apply_at": {}}

    def emit_ln_pair(p0, quantize):
        st_ = ln_state
        xg = st_.get("x_guard", 0)
        tiles = [p0, p0 + 1]
        V = lambda n, it: vt[:, n + 8 * (it % 2): n + 8 * (it % 2) + 1]
        sq_at, stats_at, sqrt_at, rstd_at, qs_at, i16_at = {}, {}, {}, {}, {}, {}
        dump = g["sqd"][:] if quantize else g["rr"][:, 0:C]
        for it in tiles:
            P.wait('a', s["x"])
            P.wait('a', s["v"], max(xg, st_["vt_free"][it % 2]))
            sq_at[it] = P.op('a', s["a"], 1, lambda e, it=it: e.activation(
                out=dump, in_=X[:, it, :], func=AF.Square, accum_out=V(1, it)))
        for it in tiles:
            P.wait('v', s["a"], sq_at[it])
            def stats(e, it=it):
                e.reduce_sum(V(2, it), X[:, it, :], axis=AX.X)
                e.tensor_scalar(V(4, it), V(1, it), INV_C, None, ALU.mult)
                e.tensor_scalar(V(3, it), V(2, it), INV_C, None, ALU.mult)
                vsep(e)
                return e.scalar_tensor_tensor(V(5, it), V(3, it), V(3, it), V(4, it),
                                              ALU.mult, ALU.subtract)
            stats_at[it] = P.op('v', s["v"], 1, stats)
        for it in tiles:
            P.wait('a', s["v"], stats_at[it])
            sqrt_at[it] = P.op('a', s["a"], 1, lambda e, it=it: e.activation(
                out=V(6, it), in_=V(5, it), func=AF.Sqrt, scale=-1.0, bias=CV(7)))
        for it in tiles:
            P.wait('v', s["a"], sqrt_at[it])
            def rstd(e, it=it):
                e.reciprocal(V(7, it), V(6, it))
                vsep(e)
                return e.scalar_tensor_tensor(V(8, it), V(3, it), -1.0, V(7, it),
                                              ALU.mult, ALU.mult)
            rstd_at[it] = P.op('v', s["v"], 1, rstd)
        if not quantize:
            for it in tiles:
                h2t = g["h2stg"][:, it % 2, :]
                P.wait('a', s["v"], rstd_at[it])
                if it >= 2:
                    P.wait('a', s["tp_h2"], 16 * (it - 1))
                a_ap = P.op('a', s["a"], 1, lambda e, it=it, h2t=h2t: e.activation(
                    out=h2t, in_=X[:, it, :], func=AF.Identity, scale=V(7, it), bias=V(8, it)))
                P.wait('sy', s["a"], a_ap)
                P.wait('sy', s["tp_hq"])
                P.wait('sy', s["tp_pq"])
                P.wait('sy', s["tp_at"])
                P.op('sy', s["tp_h2"], 16, lambda e, it=it, h2t=h2t: e.dma_start_transpose(
                    g["h2fm"][:, :, it, :], h2t))
                st_["vt_free"][it % 2] = rstd_at[it]
            return
        for it in tiles:
            hb = g["hscr"][:, it % 2, :]
            P.wait('a', s["v"], rstd_at[it])
            P.wait('a', s["v"], st_["hb_free"][it % 2])
            st_["apply_at"][it] = P.op('a', s["a"], 1, lambda e, it=it, hb=hb: e.activation(
                out=hb, in_=X[:, it, :], func=AF.Identity, scale=V(7, it), bias=V(8, it)))
        for it in tiles:
            hb = g["hscr"][:, it % 2, :]
            P.wait('v', s["a"], st_["apply_at"][it])
            def qscale(e, it=it, hb=hb):
                e.reduce_max(V(1, it), hb, axis=AX.X, apply_absolute_value=True)
                vsep(e)
                e.tensor_scalar(VC(5, it), V(1, it), 1e-5, None, ALU.max)
                vsep(e)
                e.reciprocal(V(3, it), VC(5, it))
                vsep(e)
                return e.tensor_scalar(VC(3, it), V(3, it), 127.0, None, ALU.mult)
            qs_at[it] = P.op('v', s["v"], 1, qscale)
            st_["vt_free"][it % 2] = qs_at[it]
        for it in tiles:
            hb = g["hscr"][:, it % 2, :]
            i16 = g["i16s"] if it % 2 == 0 else g["i16b"]
            P.wait('a', s["v"], qs_at[it])
            P.wait('a', s["v"], st_["i16_free"][it % 2])
            i16_at[it] = P.op('a', s["a"], 1, lambda e, hb=hb, it=it, i16=i16: e.activation(
                out=i16[:], in_=hb, func=AF.Identity, scale=VC(3, it)))
        for it in tiles:
            hb = g["hscr"][:, it % 2, :]
            i16 = g["i16s"] if it % 2 == 0 else g["i16b"]
            hq = g["hqt"][:, it % 2, :]
            P.wait('v', s["a"], i16_at[it])
            if it >= 2:
                P.wait('v', s["tp_hq"], 16 * (it - 1))
            def rnd(e, hb=hb, it=it, hq=hq, i16=i16):
                e.scalar_tensor_tensor(g["d16s"][:], hb, VC(3, it), i16[:], ALU.mult, ALU.subtract)
                e.scalar_tensor_tensor(g["t16s"][:], g["d16s"][:], 0.5, i16[:], ALU.is_gt, ALU.add)
                return e.scalar_tensor_tensor(hq, g["d16s"][:], -0.5, g["t16s"][:], ALU.is_lt, ALU.subtract)
            n = P.op('v', s["v"], 1, rnd)
            ln_state["rnd_at"][it] = n
            st_["hb_free"][it % 2] = n
            st_["i16_free"][it % 2] = n
            P.wait('sy', s["v"], n)
            P.wait('sy', s["tp_pq"])
            P.wait('sy', s["tp_h2"])
            P.wait('sy', s["tp_at"])
            P.op('sy', s["tp_hq"], 16, lambda e, it=it, hq=hq: e.dma_start_transpose(
                g["Hq_fm"][:, :, it, :], hq))

    def emit_ln(quantize):
        ln_state["vt_free"] = [0, 0]
        ln_state["hb_free"] = [0, 0]
        ln_state["i16_free"] = [0, 0]
        ln_state["x_guard"] = s["v"].n
        for p0 in range(0, NT, 2):
            emit_ln_pair(p0, quantize)

    def emit_gu(stage, b):
        wbuf, gtmp = g["wbuf"], g["gtmp"]
        quant = (stage == 0 and b == 0)
        bank_free = [s["a"].n, s["a"].n, s["v"].n, s["v"].n]
        gt_free = [s["v"].n, s["v"].n]
        slot_te = [s["te"].n, s["te"].n]
        v_guard = s["v"].n
        for j in range(NJ):
            wb = wbuf if (quant or j % 2 == 0) else g["wbuf2"]
            gslot = wb[:, 0:NKC, :]
            uslot = wb[:, NKC:2 * NKC, :]
            if quant:
                if j > 0:
                    P.wait('v', s["te"])
                    P.wait('v', s["twb"])
                emit_weight_quant_chunk(g["wg_h"], j, gslot, g["tg_h"], 0)
                emit_weight_quant_chunk(g["wu_h"], j, uslot, g["tu_h"], 1)
                slot_rdy = ('v', s["v"].n, 0)
            else:
                trds = s["trd"] if j % 2 == 0 else s["trd2"]
                P.wait('gp', s["te"], slot_te[j % 2])
                if j < 2:
                    P.wait('gp', s["v"], v_guard)
                P.op('gp', trds, 16, lambda e, j=j, gslot=gslot: e.dma_start(
                    out=gslot.rearrange("p a b -> p (a b)"),
                    in_=g["tg_h"][j].rearrange("p a b -> p (a b)")))
                P.op('gp', trds, 16, lambda e, j=j, uslot=uslot: e.dma_start(
                    out=uslot.rearrange("p a b -> p (a b)"),
                    in_=g["tu_h"][j].rearrange("p a b -> p (a b)")))
                slot_rdy = ('trd', trds, trds.n)
            if slot_rdy[0] == 'v':
                P.wait('t', s["v"], slot_rdy[1])
            else:
                P.wait('t', slot_rdy[1], slot_rdy[2])
            P.wait('t', s["tp_hq"])
            for k in range(BT):
                ti = b * BT + k
                gbank, ubank = pb[k % 2], pb[2 + (k % 2)]
                P.wait('t', s["a"], bank_free[k % 2])
                def gmm(e, ti=ti, gbank=gbank, gslot=gslot):
                    for c in range(NKC):
                        inst = e.matmul(gbank[:], g["Hq_fm"][:, c, ti, :], gslot[:, c, :],
                                        start=(c == 0), stop=(c == NKC - 1))
                    return inst
                te_g = P.op('t', s["te"], 1, gmm)
                gt = gtmp[:, k % 2, :]
                P.wait('a', s["te"], te_g)
                P.wait('a', s["v"], gt_free[k % 2])
                bank_free[k % 2] = P.op('a', s["a"], 1,
                    lambda e, gt=gt, ti=ti, gbank=gbank: e.activation(
                        out=gt, in_=gbank[:], func=AF.Silu, scale=VC(0, ti)))
                P.wait('t', s["v"], bank_free[2 + k % 2])
                def umm(e, ti=ti, ubank=ubank, uslot=uslot):
                    for c in range(NKC):
                        inst = e.matmul(ubank[:], g["Hq_fm"][:, c, ti, :], uslot[:, c, :],
                                        start=(c == 0), stop=(c == NKC - 1))
                    return inst
                te_u = P.op('t', s["te"], 1, umm)
                P.wait('v', s["te"], te_u)
                P.wait('v', s["a"], bank_free[k % 2])
                n = P.op('v', s["v"], 1, lambda e, j=j, k=k, ti=ti, ubank=ubank, gt=gt:
                         e.scalar_tensor_tensor(g["arena"][:, j, k, :], ubank[:], VC(1, ti),
                                                gt, ALU.mult, ALU.mult))
                bank_free[2 + k % 2] = n
                gt_free[k % 2] = n
            slot_te[j % 2] = s["te"].n

    def emit_p(stage, b, tp_pq_prev):
        arena, pqtmp = g["arena"], g["pqtmp"]
        for k in range(BT):
            ti = b * BT + k
            P.wait('v', s["v"])
            def q2s(e, k=k, ti=ti):
                e.reduce_max(VTc(1), arena[:, :, k, :], axis=AX.XY, apply_absolute_value=True)
                vsep(e)
                e.tensor_scalar(VTc(2), VTc(1), 1e-5, None, ALU.max)
                vsep(e)
                e.reciprocal(VTc(3), VTc(2))
                e.scalar_tensor_tensor(VC(2, ti), VTc(2), 0.5 / 127.0, wsc[:, 2:3],
                                       ALU.mult, ALU.mult)
                return e.tensor_scalar(VC(4, ti), VTc(3), 127.0, None, ALU.mult)
            P.op('v', s["v"], 1, q2s)
        q2_done = s["v"].n
        seq = [(j, k) for j in range(NJ) for k in range(BT)]
        a_at, v_at = {}, {}
        def emit_cast(n):
            j, k = seq[n]
            ti = b * BT + k
            i16 = g["i16s"][:, (n % 2) * 512:(n % 2) * 512 + 512]
            P.wait('a', s["v"], max(q2_done, v_at.get(n - 2, 0)))
            a_at[n] = P.op('a', s["a"], 1, lambda e, j=j, k=k, ti=ti, i16=i16: e.activation(
                out=i16, in_=arena[:, j, k, :], func=AF.Identity, scale=VC(4, ti)))
        def emit_rnd(n):
            j, k = seq[n]
            ti = b * BT + k
            i16 = g["i16s"][:, (n % 2) * 512:(n % 2) * 512 + 512]
            pqt = g["pqtmp"] if j % 2 == 0 else g["pqtmp2"]
            P.wait('v', s["a"], a_at[n])
            if k == 0:
                P.wait('v', s["tp_pq"], tp_pq_prev[j % 2])
            def rnd(e, j=j, k=k, ti=ti, i16=i16, pqt=pqt):
                e.scalar_tensor_tensor(g["d16s"][:, 0:512], arena[:, j, k, :], VC(4, ti),
                                       i16, ALU.mult, ALU.subtract)
                e.scalar_tensor_tensor(g["t16s"][:, 0:512], g["d16s"][:, 0:512], 0.5,
                                       i16, ALU.is_gt, ALU.add)
                return e.scalar_tensor_tensor(pqt[:, k, :], g["d16s"][:, 0:512], -0.5,
                                              g["t16s"][:, 0:512], ALU.is_lt, ALU.subtract)
            v_at[n] = P.op('v', s["v"], 1, rnd)
            if k == BT - 1:
                P.wait('sy', s["v"], v_at[n])
                P.wait('sy', s["tp_pq"])
                P.wait('sy', s["tp_hq"])
                P.wait('sy', s["tp_h2"])
                P.wait('sy', s["tp_at"])
                tp_pq_prev[j % 2] = s["tp_pq"].n + 16
                P.op('sy', s["tp_pq"], 16, lambda e, j=j, pqt=pqt: e.dma_start_transpose(
                    arena[:, j].rearrange("p k (c q) -> p k c q", c=4), pqt[:]))
        for n in range(len(seq) + 1):
            if n < len(seq):
                emit_cast(n)
            if n >= 1:
                emit_rnd(n - 1)

    def emit_d(stage, b):
        wdbuf, arena = g["wdbuf"], g["arena"]
        quant = (stage == 0 and b == 0)
        bank_free = [s["v"].n, s["v"].n]
        dslot_te = [s["te"].n, s["te"].n]
        v_guard_d = s["v"].n
        for fc in range(NFD):
            wdb = wdbuf if (quant or fc % 2 == 0) else g["wdbuf2"]
            if quant:
                P.wait('v', s["te"])
                P.wait('v', s["twb"])
                emit_weight_quant_chunk(g["wd_h"], fc, wdb[:], g["td_h"], 2)
                P.wait('t', s["v"])
            else:
                trds = s["trd"] if fc % 2 == 0 else s["trd2"]
                P.wait('gp', s["te"], dslot_te[fc % 2])
                if fc < 2:
                    P.wait('gp', s["v"], v_guard_d)
                P.op('gp', trds, 16, lambda e, fc=fc, wdb=wdb: e.dma_start(
                    out=wdb[:].rearrange("p a b -> p (a b)"),
                    in_=g["td_h"][fc].rearrange("p a b -> p (a b)")))
                P.wait('t', trds)
            P.wait('t', s["tp_pq"])
            slotv = arena[:].rearrange("p j k (c q) -> p j k c q", c=4)
            for k in range(BT):
                ti = b * BT + k
                ybank = pb[4 + (k % 2)]
                P.wait('t', s["v"], bank_free[k % 2])
                def dmm(e, k=k, ybank=ybank, slotv=slotv, fc=fc, wdb=wdb):
                    for kc in range(32):
                        inst = e.matmul(ybank[:, 0:256], slotv[:, kc // 4, k, kc % 4, :],
                                        wdb[:, kc, :], start=(kc == 0), stop=(kc == 31))
                    return inst
                te_y = P.op('t', s["te"], 1, dmm)
                xsl = X[:, ti, fc * 256:(fc + 1) * 256]
                P.wait('v', s["te"], te_y)
                bank_free[k % 2] = P.op('v', s["v"], 1,
                    lambda e, ybank=ybank, ti=ti, xsl=xsl: e.scalar_tensor_tensor(
                        xsl, ybank[:, 0:256], VC(2, ti), xsl, ALU.mult, ALU.add))
            dslot_te[fc % 2] = s["te"].n

    def emit_ffn(stage):
        if stage == 0:
            units = []
            for srch, col in ((g["wg_h"], 0), (g["wu_h"], 1), (g["wd_h"], 2)):
                u, fin = absum_units(srch, col)
                units.append((u, fin))
            ln_state["vt_free"] = [0, 0]
            ln_state["hb_free"] = [0, 0]
            ln_state["i16_free"] = [0, 0]
            ln_state["x_guard"] = s["v"].n
            flat = [u for us, _ in units for u in us]
            fins = {len(units[0][0]) - 1: units[0][1],
                    len(units[0][0]) + len(units[1][0]) - 1: units[1][1],
                    len(units[0][0]) + len(units[1][0]) + len(units[2][0]) - 1: units[2][1]}
            pair_i = [0]
            for i, u in enumerate(flat):
                u()
                if i in fins:
                    fins[i]()
                if i % 3 == 2 and pair_i[0] < NT // 2:
                    emit_ln_pair(2 * pair_i[0], True)
                    pair_i[0] += 1
            while pair_i[0] < NT // 2:
                emit_ln_pair(2 * pair_i[0], True)
                pair_i[0] += 1
        else:
            emit_ln(quantize=True)
        def batch_scales(e):
            vsep(e)
            e.tensor_scalar(vec[:, 0:NT], vec[:, 5 * NT:6 * NT], wsc[:, 0:1], 1.0 / 127.0,
                            ALU.mult, ALU.mult)
            return e.tensor_scalar(vec[:, NT:2 * NT], vec[:, 5 * NT:6 * NT], wsc[:, 1:2],
                                   1.0 / 127.0, ALU.mult, ALU.mult)
        P.op('v', s["v"], 1, batch_scales)
        if CUT == 'absum' or CUT == 'ln':
            if DBG:
                dbg_dump("dbg_wsc", wsc[:])
                dbg_dump("dbg_hq", g["Hq_fm"][:])
                dbg_dump("dbg_vec", vec[:])
            return
        tp_pq_prev = [0, 0]
        for b in range(NB):
            emit_gu(stage, b)
            if stage == 0 and b == 0 and DBG:
                dbg_dump("dbg_wg", g["wbuf"][:])
                dbg_dump("dbg_arena", g["arena"][:])
            if CUT == 'gu0' and b == 0:
                return
            emit_p(stage, b, tp_pq_prev)
            if stage == 0 and b == 0 and DBG:
                dbg_dump("dbg_pq", g["arena"][:])
                dbg_dump("dbg_vec2", vec[:])
            if CUT == 'p0' and b == 0:
                return
            emit_d(stage, b)
            if stage == 0 and b == 0 and CUT == 'd0':
                return

    def emit_rope_tables():
        iota, cosT, sinT = g["iota"], g["cosT"], g["sinT"]
        ru, rk, rd, rt, rr = g["ru"], g["rk"], g["rd"], g["rt"], g["rr"]
        P.wait('v', s["cst"])
        def theta(e):
            e.tensor_scalar(ru[:], iota[:], CV(1), None, ALU.mult)
            return e.scalar_tensor_tensor(rr[:], iota[:], CV(0), ru[:], ALU.mult, ALU.add)
        P.op('v', s["v"], 1, theta)
        for phase, dst, signed in ((0.0, sinT, True), (float(np.pi / 2), cosT, False)):
            def reduce2(e, phase=phase):
                e.tensor_scalar(ru[:], rr[:], INV2PI, phase * INV2PI, ALU.mult, ALU.add)
                e.tensor_copy(rk[:], ru[:])
                e.tensor_tensor(rd[:], ru[:], rk[:], ALU.subtract)
                e.scalar_tensor_tensor(rt[:], rd[:], 0.5, rk[:], ALU.is_gt, ALU.add)
                e.scalar_tensor_tensor(rt[:], rd[:], -0.5, rt[:], ALU.is_lt, ALU.subtract)
                e.scalar_tensor_tensor(ru[:], rt[:], C_HI, rr[:], ALU.mult, ALU.add)
                e.tensor_scalar(ru[:], ru[:], 1.0, phase, ALU.mult, ALU.add)
                e.scalar_tensor_tensor(ru[:], rt[:], C_LO, ru[:], ALU.mult, ALU.add)
                return e.tensor_scalar(ru[:], ru[:], -PI_, PI_, ALU.max, ALU.min)
            P.op('v', s["v"], 1, reduce2)
            P.wait('a', s["v"])
            P.op('a', s["a"], 1, lambda e, dst=dst: e.activation(
                out=dst[:], in_=ru[:], func=AF.Sin))
            P.wait('v', s["a"])
            if signed:
                P.op('v', s["v"], 1, lambda e, dst=dst: e.tensor_scalar(
                    dst[:], dst[:], CV(2), None, ALU.mult))

    def emit_attn():
        av_evac = {}
        emit_rope_tables()
        emit_ln(quantize=False)
        strips = g["strips"]
        qw_rdy_l = [0] * (NG + 1)
        qkv_te = [0] * NG
        scores_te_prev = [0]
        def load_group(grp, wait_te):
            qkwb = g["qkw"] if grp % 2 == 0 else g["qkw2"]
            vwb = g["vw"] if grp % 2 == 0 else g["vw2"]
            qws = s["qw"] if grp % 2 == 0 else s["trd"]
            if wait_te is None:
                P.wait('gp', s["te"])
                P.wait('gp', s["v"])
            else:
                P.wait('gp', s["te"], wait_te)
            P.op('gp', qws, 16, lambda e, grp=grp, qkwb=qkwb: e.dma_start(
                out=qkwb[:, :, 0:128], in_=g["wqkv_h"][grp]))
            P.op('gp', qws, 16, lambda e, grp=grp, qkwb=qkwb: e.dma_start(
                out=qkwb[:, :, 128:256], in_=g["wqkv_h"][NG + grp]))
            n = P.op('gp', qws, 16, lambda e, grp=grp, vwb=vwb: e.dma_start(
                out=vwb[:], in_=g["wqkv_h"][2 * NG + grp]))
            qw_rdy_l[grp] = (qws, n)
        load_group(0, None)
        for grp in range(NG):
            qkwb = g["qkw"] if grp % 2 == 0 else g["qkw2"]
            vwb = g["vw"] if grp % 2 == 0 else g["vw2"]
            qws, qw_rdy = qw_rdy_l[grp]
            iters = [(sel, tch) for sel in (0, 1) for tch in range(4)]
            qk_free = [s["v"].n, s["v"].n]
            st_free = [s["a"].n, s["a"].n]
            sw_free = [s["v"].n, s["v"].n]
            rn_free = [s["v"].n, s["v"].n]
            te_qk = {}
            a_evac = {}
            te_stats = {}
            a_rn = {}
            def emit_qk_mm(i):
                sel, tch = iters[i]
                qk = pb[i % 2]
                P.wait('t', qws, qw_rdy)
                P.wait('t', s["tp_h2"])
                P.wait('t', s["v"], qk_free[i % 2])
                def qkmm(e, sel=sel, tch=tch, qk=qk, qkwb=qkwb):
                    rhs = g["h2fm"][:, :, 4 * tch:4 * tch + 4, :]
                    for c in range(NKC):
                        inst = e.matmul(qk[:], qkwb[:, c, sel * 128:(sel + 1) * 128],
                                        rhs[:, c].rearrange("p i q -> p (i q)"),
                                        start=(c == 0), stop=(c == NKC - 1))
                    return inst
                te_qk[i] = P.op('t', s["te"], 1, qkmm)
            def emit_evac(i):
                sel, tch = iters[i]
                qk = pb[i % 2]
                stb, swb, rnb = pb[2 + i % 2], pb[4 + i % 2], pb[6 + i % 2]
                sqb = g["sqb"][:, i % 2, :]
                qksb = g["qksb"][:, i % 2, :]
                P.wait('a', s["te"], te_qk[i])
                P.wait('a', s["te"], te_stats.get(i - 2, 0))
                P.wait('a', s["v"], av_evac.get("v", 0))
                P.op('a', s["a"], 1, lambda e, qk=qk, sqb=sqb: e.activation(
                    out=sqb, in_=qk[:], func=AF.Square))
                a_evac[i] = P.op('a', s["a"], 1, lambda e, qk=qk, qksb=qksb: e.activation(
                    out=qksb, in_=qk[:], func=AF.Copy))
                P.wait('t', s["a"], a_evac[i])
                P.wait('t', s["a"], st_free[i % 2])
                P.op('t', s["te"], 1, lambda e, stb=stb, sqb=sqb: e.matmul(
                    stb[:], g["blkw"][:], sqb, start=True, stop=True))
                P.wait('t', s["v"], sw_free[i % 2])
                te_stats[i] = P.op('t', s["te"], 1, lambda e, swb=swb, qksb=qksb: e.matmul(
                    swb[:], g["pswap"][:], qksb, start=True, stop=True))
                P.wait('a', s["te"], te_stats[i])
                P.wait('a', s["v"], rn_free[i % 2])
                P.op('a', s["a"], 1, lambda e, stb=stb, rnb=rnb: e.activation(
                    out=rnb[:], in_=stb[:], func=AF.Ln, scale=1.0 / 64.0, bias=CV(5)))
                st_free[i % 2] = s["a"].n
                a_rn[i] = P.op('a', s["a"], 1, lambda e, rnb=rnb: e.activation(
                    out=rnb[:], in_=rnb[:], func=AF.Exp, scale=-0.5))
                dstbuf = g["Qn"] if sel == 0 else g["Kn"]
                wcol = 3 if sel == 0 else 4
                tsl = slice(tch * 512, (tch + 1) * 512)
                ra = g["ra"][:, i % 2, :]
                P.wait('v', s["a"], a_rn[i])
                P.wait('v', s["te"], scores_te_prev[0])
                def rope(e, qk=qk, swb=swb, rnb=rnb, tsl=tsl, wcol=wcol, dstbuf=dstbuf, ra=ra):
                    e.tensor_tensor(ra, qk[:], g["cosT"][:, tsl], ALU.mult)
                    e.tensor_tensor(g["rb"][:], swb[:], g["sinT"][:, tsl], ALU.mult)
                    e.tensor_add(ra, ra, g["rb"][:])
                    return e.scalar_tensor_tensor(dstbuf[:, tsl], ra, CV(wcol), rnb[:],
                                                  ALU.mult, ALU.mult)
                n = P.op('v', s["v"], 1, rope)
                qk_free[i % 2] = n
                sw_free[i % 2] = n
                rn_free[i % 2] = n
            for i in range(len(iters) + 1):
                if i < len(iters):
                    emit_qk_mm(i)
                if i >= 1:
                    emit_evac(i - 1)
            for ti in range(NT):
                vbank = pb[ti % 2]
                P.wait('t', s["v"])
                def vmm(e, ti=ti, vbank=vbank, vwb=vwb):
                    for c in range(NKC):
                        inst = e.matmul(vbank[:, 0:128], g["h2fm"][:, c, ti, :],
                                        vwb[:, c, :], start=(c == 0), stop=(c == NKC - 1))
                    return inst
                te_v = P.op('t', s["te"], 1, vmm)
                P.wait('v', s["te"], te_v)
                def vevac(e, ti=ti, vbank=vbank):
                    e.tensor_copy(g["Vext"][:, ti, :].rearrange("p (h d) -> p h d", d=65)[:, :, 0:64],
                                  vbank[:, 0:128].rearrange("p (h d) -> p h d", d=64))
                    return e.memset(g["Vext"][:, ti, 64::65], 1.0)
                P.op('v', s["v"], 1, vevac)
            if grp + 1 < NG:
                load_group(grp + 1, s["te"].n)
            chunk_list = [(e_i, tch) for e_i in range(2) for tch in range(4)]
            av_last = {}
            def emit_sav(ci):
                e_i, tch = chunk_list[ci]
                ci_ = ci
                ntk = 4 * (tch + 1)
                qsl = slice(tch * 512, (tch + 1) * 512)
                avb = pb[2 + ci % 2]
                exp_at = {}
                av_at = {}
                def emit_s(tkt):
                    diag = tkt - 4 * tch
                    sb = pb[tkt % 2]
                    if tkt >= 2:
                        P.wait('t', s["a"], exp_at[tkt - 2])
                    ksl = slice(tkt * 128, (tkt + 1) * 128)
                    if tkt < 2 and ci > 0:
                        pass
                    def smm(e, e_i=e_i, ksl=ksl, qsl=qsl, sb=sb, diag=diag):
                        i1 = e.matmul(sb[:], g["Kn"][64 * e_i:64 * (e_i + 1), ksl],
                                      g["Qn"][64 * e_i:64 * (e_i + 1), qsl],
                                      start=True, stop=(diag < 0))
                        if diag >= 0:
                            i1 = e.matmul(sb[:], g["ident"][:], strips[:, diag, :],
                                          start=False, stop=True)
                        return i1
                    te_s = P.op('t', s["te"], 1, smm)
                    P.wait('a', s["te"], te_s)
                    if tkt >= 2:
                        P.wait('a', s["te"], av_at[tkt - 2])
                    elif ci > 0:
                        P.wait('a', s["te"], av_last[ci - 1])
                    P.op('a', s["a"], 1, lambda e, tkt=tkt, sb=sb: e.activation(
                        out=g["expS"][:, tkt % 2, :], in_=sb[:], func=AF.Exp,
                        scale=0.125, bias=CV(6)))
                    exp_at[tkt] = s["a"].n
                def emit_av(tkt):
                    P.wait('t', s["a"], exp_at[tkt])
                    if tkt == 0:
                        P.wait('t', s["v"], av_evac.get(("av", ci % 2), 0))
                    def avmm(e, e_i=e_i, tkt=tkt, ntk=ntk, avb=avb):
                        return e.matmul(avb[0:65, :],
                                        g["Vext"][:, tkt, e_i * 65:(e_i + 1) * 65],
                                        g["expS"][:, tkt % 2, :],
                                        start=(tkt == 0), stop=(tkt == ntk - 1))
                    av_at[tkt] = P.op('t', s["te"], 1, avmm)
                for tkt in range(ntk + 1):
                    if tkt < ntk:
                        emit_s(tkt)
                    if tkt >= 1:
                        emit_av(tkt - 1)
                av_last[ci] = av_at[ntk - 1]
            def emit_evac(ci):
                e_i, tch = chunk_list[ci]
                head = 2 * grp + e_i
                avb = pb[2 + ci % 2]
                lsb = g["rb"][0:1, :]
                P.wait('a', s["te"], av_last[ci])
                P.wait('a', s["v"], av_evac.get("lsb", 0))
                P.op('a', s["a"], 1, lambda e, avb=avb: e.activation(
                    out=lsb, in_=avb[64:65, :], func=AF.Ln))
                a_ln = P.op('a', s["a"], 1, lambda e: e.activation(
                    out=lsb, in_=lsb, func=AF.Exp, scale=-1.0))
                lb_i = 6 + ci % 2
                lbb = pb[lb_i]
                P.wait('t', s["a"], a_ln)
                P.wait('t', s["v"], av_evac.get(("lb", lb_i), 0))
                te_lb = P.op('t', s["te"], 1, lambda e, lbb=lbb: e.matmul(
                    lbb[0:64, :], g["ones64f"][:], lsb, start=True, stop=True))
                av_evac["lsb"] = s["v"].n
                P.wait('v', s["te"], te_lb)
                linvb = g["ra"][0:64, ci % 2, :]
                P.op('v', s["v"], 1, lambda e, lbb=lbb, linvb=linvb: e.tensor_copy(linvb, lbb[0:64, :]))
                av_evac[("lb", lb_i)] = s["v"].n
                av_evac["lsb"] = s["v"].n
                dst = g["attn_fm"][(head % 2) * 64:(head % 2) * 64 + 64, head // 2,
                                   4 * tch:4 * tch + 4, :]
                if head % 2 == 0:
                    n = P.op('v', s["v"], 1, lambda e, avb=avb, linvb=linvb, dst=dst: e.tensor_tensor(
                        dst.rearrange("p i q -> p (i q)"), avb[0:64, :], linvb, ALU.mult))
                    av_evac[("av", ci % 2)] = n
                    av_evac["v"] = n
                else:
                    nsb = g["qksb"][:, ci % 2, :][0:64, :]
                    P.op('v', s["v"], 1, lambda e, avb=avb, linvb=linvb, nsb=nsb: e.tensor_tensor(
                        nsb, avb[0:64, :], linvb, ALU.mult))
                    v_ns = s["v"].n
                    av_evac[("av", ci % 2)] = v_ns
                    prm_i = 6 + (ci + 1) % 2
                    prm = pb[prm_i]
                    P.wait('t', s["v"], max(v_ns, av_evac.get(("lb", prm_i), 0)))
                    te_pm = P.op('t', s["te"], 1, lambda e, nsb=nsb, prm=prm: e.matmul(
                        prm[:], g["shift64"][:], nsb, start=True, stop=True))
                    P.wait('v', s["te"], te_pm)
                    n = P.op('v', s["v"], 1, lambda e, prm=prm, dst=dst: e.tensor_copy(
                        dst.rearrange("p i q -> p (i q)"), prm[64:128, :]))
                    av_evac["v"] = n
                    av_evac[("lb", prm_i)] = n
            for ci in range(len(chunk_list) + 1):
                if ci < len(chunk_list):
                    emit_sav(ci)
                if ci >= 1:
                    emit_evac(ci - 1)
            scores_te_prev[0] = s["te"].n
        bank_free = [s["v"].n, s["v"].n]
        for fc in range(2):
            P.wait('gp', s["te"])
            qw_rdy = P.op('gp', s["qw"], 16, lambda e, fc=fc: e.dma_start(
                out=g["wob"][:], in_=g["wo_h"][fc]))
            for it in range(NT):
                yb = pb[2 + (it % 2)]
                P.wait('t', s["qw"], qw_rdy)
                P.wait('t', s["tp_at"])
                P.wait('t', s["v"], bank_free[it % 2])
                def omm(e, it=it, yb=yb):
                    for c in range(NKC):
                        inst = e.matmul(yb[:], g["attn_fm"][:, c, it, :], g["wob"][:, c, :],
                                        start=(c == 0), stop=(c == NKC - 1))
                    return inst
                te_y = P.op('t', s["te"], 1, omm)
                xsl = X[:, it, fc * 512:(fc + 1) * 512]
                P.wait('v', s["te"], te_y)
                bank_free[it % 2] = P.op('v', s["v"], 1,
                    lambda e, yb=yb, xsl=xsl: e.tensor_add(xsl, xsl, yb[:]))

    if DO_A:
        emit_ffn(0)
    if DO_B:
        emit_attn()
    if DO_C:
        emit_ffn(2)
    ov = g["out_h"].rearrange("(i p) c -> p i c", p=128)
    P.wait('sy', s["v"])
    P.wait('sy', s["a"])
    P.wait('sy', s["te"])
    P.op('sy', s["stm"], 16, lambda e: e.dma_start(out=ov, in_=X[:]))
    P.wait('sy', s["stm"])

    with nc.Block() as block:
        _fill_block(block, P)
    return P


def _fill_block(block, P):
    @block.sync
    def _(e):
        P.run('sy', e)
    @block.gpsimd
    def _(e):
        P.run('gp', e)
    @block.vector
    def _(e):
        P.run('v', e)
    @block.scalar
    def _(e):
        P.run('a', e)
    @block.tensor
    def _(e):
        P.run('t', e)


_NC_CACHE = {}

def _get_nc():
    key = (DO_A, DO_B, DO_C, CUT, DBG)
    if key not in _NC_CACHE:
        nc, st, g = build_program()
        emit_all(nc, st, g)
        _NC_CACHE[key] = (nc, st, g)
    return _NC_CACHE[key]


def kernel(L0_attn_x, L0_attn_w_qkv, L0_attn_w_out, L0_attn_q_norm_w,
           L0_attn_k_norm_w, L0_ffn_w_gate, L0_ffn_w_up, L0_ffn_w_down):
    consts = host_prepare(L0_attn_w_qkv, L0_attn_w_out, L0_attn_q_norm_w,
                          L0_attn_k_norm_w, L0_ffn_w_gate, L0_ffn_w_up, L0_ffn_w_down)
    x = np.ascontiguousarray(np.asarray(L0_attn_x, np.float32))
    nc, _st, _g = _get_nc()
    in_maps = []
    for b in range(8):
        m = {"x": np.ascontiguousarray(x[b])}
        m.update(consts)
        in_maps.append(m)
    res = run_bass_kernel_spmd(nc, in_maps, core_ids=list(range(8)))
    out = np.stack([np.asarray(r["out"]) for r in res.results], axis=0)
    return out.astype(np.float32)
